# version 13
# baseline (speedup 1.0000x reference)
import numpy as np
import ml_dtypes
import concourse.bass as bass
import concourse.mybir as mybir
from concourse.bass_utils import run_bass_kernel_spmd

F32 = mybir.dt.float32
BF16 = mybir.dt.bfloat16
I32 = mybir.dt.int32
AF = mybir.ActivationFunctionType
ALU = mybir.AluOpType
AX = mybir.AxisListType

T = 2048
D = 1024
NTT = T // 128
NCC = D // 128
NH = 16
HD = 64
NE = 8
HF = 2048
DBG = []


class Marks:

    def __init__(self, prog=None):
        self.prog = prog
        self.counts = {}
        self.marks = {}
        self.dry = True

    def reset_pass(self, dry):
        self.dry = dry
        self.counts = {}

    def bump(self, sem, n):
        self.counts[sem] = self.counts.get(sem, 0) + n

    def mark(self, name, sem):
        c = self.counts.get(sem, 0)
        if sem in ("qL", "qA"):
            self.prog.qstate[sem]["mark"] = c
        if self.dry:
            self.marks[name] = (sem, c)
        else:
            assert self.marks[name] == (sem, c), (name, self.marks[name], (sem, c))

    def at(self, name):
        return self.marks.get(name, (None, 0))


AUTO_CHAIN = {"vector": "vS", "scalar": "aS", "gpsimd": "pS"}


class Eng:

    def __init__(self, prog, name):
        self.prog = prog
        self.name = name
        self.h = None

    def op(self, fname, *args, **kw):
        auto = AUTO_CHAIN.get(self.name) if fname != "dma_start" else None
        if fname == "dma_start" and self.name in ("sync", "scalar"):
            qs = "qL" if self.name == "sync" else "qA"
            st = self.prog.qstate[qs]
            if st["mark"] > st["barrier"]:
                if not self.prog.dry:
                    self.h.wait_ge(self.prog.sems[qs], st["mark"])
                st["barrier"] = st["mark"]
        if auto is not None:
            cnt = self.prog.M.counts.get(auto, 0)
            if not self.prog.dry and cnt:
                self.h.wait_ge(self.prog.sems[auto], cnt)
        if self.prog.dry:
            inst = _DryInst(self.prog)
        else:
            inst = _RealInst(self.prog, getattr(self.h, fname)(*args, **kw))
        if auto is not None:
            self.prog.M.bump(auto, 1)
            if not self.prog.dry:
                inst.inst.then_inc(self.prog.sems[auto], 1)
        return inst

    def wait(self, markname, extra=0):
        sem, cnt = self.prog.M.at(markname)
        if not self.prog.dry:
            self.h.wait_ge(self.prog.sems[sem], cnt + extra)

    def wait_abs(self, sem, value):
        if not self.prog.dry:
            self.h.wait_ge(self.prog.sems[sem], value)


class _DryInst:
    def __init__(self, prog):
        self.prog = prog

    def inc(self, sem, n=1):
        if n and sem not in ("vS", "aS", "pS"):
            self.prog.M.bump(sem, n)
        return self


class _RealInst:
    def __init__(self, prog, inst):
        self.prog = prog
        self.inst = inst

    def inc(self, sem, n=1):
        if n and sem not in ("vS", "aS", "pS"):
            self.prog.M.bump(sem, n)
            self.inst.then_inc(self.prog.sems[sem], n)
        return self


class MK:

    def __init__(self, dbg=()):
        self.dbg = list(dbg)
        self.M = Marks(self)
        self.dry = True
        self.sems = {}
        self.nc = None
        self.qstate = {}

    def declare(self):
        nc = self.nc
        A = lambda *a, **k: nc.dram_tensor(*a, **k).ap()
        self.i_x = A("x_in", [T, D], F32, kind="ExternalInput")
        for nm, shp in [
            ("w_in", [2 * D, D]), ("w_delta", [D, D]), ("w_B", [D, D]),
            ("w_C", [D, D]), ("w_out", [D, D]), ("qkv_w", [3 * D, D]),
            ("out_w", [D, D]), ("gate_w", [NE, D]),
            ("w1", [NE, HF, D]), ("w2", [NE, HF, D]), ("w3", [NE, D, HF]),
            ("vecs_a", [49, D]), ("b_out_m", [D]), ("out_b_f", [D]),
            ("b3_all", [NE, D]),
        ]:
            setattr(self, "i_" + nm, A(nm, shp, F32, kind="ExternalInput"))
        for nm, shp in [
            ("identb", [128, 128]), ("sel16", [16, 16]), ("onesr", [1, 128]),
            ("onesc", [128, 1]), ("permT", [128, 128]), ("onesq", [128, 128]),
            ("sel2", [128, 2]),
        ]:
            setattr(self, "i_" + nm, A(nm, shp, BF16, kind="ExternalInput"))
        for nm, shp in [
            ("identf", [128, 128]), ("onesrf", [1, 128]), ("jcol", [128, 1]),
            ("trow", [1, T]), ("scals", [1, 8]),
        ]:
            setattr(self, "i_" + nm, A(nm, shp, F32, kind="ExternalInput"))
        self.s_w_in = A("s_w_in", [2 * D, D], BF16)
        self.s_w_delta = A("s_w_delta", [D, D], BF16)
        self.s_w_B = A("s_w_B", [D, D], BF16)
        self.s_w_C = A("s_w_C", [D, D], BF16)
        self.s_w_out = A("s_w_out", [D, D], BF16)
        self.s_qkv = A("s_qkv", [8, 384, D], BF16)
        self.s_out_w = A("s_out_w", [D, D], BF16)
        self.s_w1 = A("s_w1", [NE, HF, D], BF16)
        self.s_w2 = A("s_w2", [NE, HF, D], BF16)
        self.s_w3 = A("s_w3", [NE, D, HF], BF16)
        self.s_xn = A("s_xn", [T, D], F32)
        self.s_sgv = A("s_sgv", [D, T], BF16)
        self.s_s = A("s_s", [T, D], F32)
        self.s_t1 = A("s_t1", [T, D], F32)
        self.o_out = A("out", [T, D], F32, kind="ExternalOutput")
        self.o_dbg = {}
        for nm, shp, dt in self.dbg:
            self.o_dbg[nm] = A("dbg_" + nm, shp, dt, kind="ExternalOutput")

    def alloc(self):
        nc = self.nc
        self._guards = []

        def sb(name, shape, dtype):
            g = nc.sbuf_tensor(name, shape, dtype)
            h = g.__enter__()
            self._guards.append(g)
            return h

        self.identb = sb("c_identb", [128, 128], BF16)
        self.identf = sb("c_identf", [128, 128], F32)
        self.sel16 = sb("c_sel16", [16, 16], BF16)
        self.SEL2 = sb("c_sel2", [128, 2], BF16)
        self.onesr = sb("c_onesr", [1, 128], BF16)
        self.onesrf = sb("c_onesrf", [1, 128], F32)
        self.onesc = sb("c_onesc", [128, 1], BF16)
        self.permT = sb("c_permT", [128, 128], BF16)
        self.onesq = sb("c_onesq", [128, 128], BF16)
        self.jcol = sb("c_jcol", [128, 1], F32)
        self.SV = sb("SV", [128, 8, 56], F32)
        self.SC = sb("SC", [128, 16], F32)
        self.SR = sb("SR", [1, 128], F32)
        self.GW = sb("GW", [128, 8, 8], BF16)
        self.NRM = sb("NRM", [128, 2, 1024], BF16)
        self.MH = sb("MH", [128, 18], F32)
        self.E16 = sb("E16", [16, 256], BF16)
        self.STC = sb("STC", [128, 48], F32)

        self.BR = sb("BR", [1, 3, D], BF16)
        self.YS = sb("YS", [1, T], F32)
        self.RB = sb("RB", [1, T], BF16)
        self.RT = sb("RT", [128, 2, T], BF16)
        self.FM4 = sb("FM4", [128, NCC, T], BF16)
        self.SLA = sb("SLA", [128, 16, 1024], BF16)
        self.SLB = sb("SLB", [128, 16, 1024], BF16)
        self.BIGA = sb("BIGA", [128, 16384], BF16)
        self.DEC = sb("DEC", [128, T], F32)
        self.INP = sb("INP", [128, T], BF16)
        self.HS = sb("HS", [128, T], BF16)
        self.HC = sb("HC", [128, T], BF16)
        self.PT2 = sb("PT2", [128, T], BF16)
        self.UST = sb("UST", [128, 2, 1024], BF16)
        self.XT = sb("XT", [128, 2, 1024], F32)
        self.ST = sb("ST", [128, 2, 1024], F32)
        pg = nc.psum_tensor("PS", [128, 4096], F32)
        self.PS = pg.__enter__()
        self._guards.append(pg)

        for s in ["qL", "qA", "qG", "tS", "vS", "aS", "pS"]:
            g = nc.semaphore(s)
            self.sems[s] = g.__enter__()
            self._guards.append(g)

        slbF = self.SLB[:, :, :].rearrange("p a b -> p (a b)")
        self.QKT = slbF[:, 0:6144].rearrange("p (s i o) -> p s i o", s=2, i=8)
        self.VPs = [slbF[:, 6144:8256].rearrange("p (a b) -> p a b", a=16),
                    slbF[:, 12352:14464].rearrange("p (a b) -> p a b", a=16)]
        self.QA = slbF[:, 8256:10304]
        self.KA = slbF[:, 10304:12352]
        self.GL = self.RT[:, :, :].rearrange("p a b -> p (a b)").bitcast(F32)[:, 0:640]
        self.CBR = self.RB[0:1, :]
        self.DUMP = self.PT2[:, 0:1024]
        self.XTf = self.XT[:, :, :].rearrange("p a b -> p (a b)")
        self.STf = self.ST[:, :, :].rearrange("p a b -> p (a b)")
        self.DECb = self.DEC[:, :].bitcast(BF16)
        self.biga8 = self.BIGA[:, :].rearrange("p (c t) -> p c t", c=8)
        self.biga16 = self.BIGA[:, :].rearrange("p (c t) -> p c t", c=16)
        self.slaV = self.SLA[:, :, :].rearrange("p a b -> p (a b)").rearrange(
            "p (c t) -> p c t", c=8)
        self.slbV = self.SLB[:, :, :].rearrange("p a b -> p (a b)").rearrange(
            "p (c t) -> p c t", c=8)

    def bank(self, i, n=1):
        return self.PS[:, 512 * i: 512 * (i + n)]

    def qg_dma(self, E, out, in_, accum=None, markname=None, depth=1):
        E.wait_abs("qG", 16 * max(0, self._qg - (depth - 1)))
        kw = {}
        if accum is not None:
            kw["accum_op"] = accum
        E.op("dma_start", out=out, in_=in_, **kw).inc("qG", 16)
        self._qg += 1
        if markname:
            self.M.mark(markname, "qG")

    def sv(self, chunk, col):
        return self.SV[:, chunk, col: col + 1]


    def p0(self, E):
        P, nm = self, E.name
        qL, qA, qG, tS, vS, aS, pS = "qL", "qA", "qG", "tS", "vS", "aS", "pS"
        M = self.M
        ROWS = [6, 7, 0, 1]
        if nm == "sync":
            for dst, srcap in [
                (P.identb[:], P.i_identb), (P.identf[:], P.i_identf),
                (P.sel16[:], P.i_sel16), (P.onesr[:], P.i_onesr),
                (P.onesrf[:], P.i_onesrf), (P.onesc[:], P.i_onesc),
                (P.permT[:], P.i_permT), (P.jcol[:], P.i_jcol),
                (P.onesq[:], P.i_onesq), (P.SEL2[:], P.i_sel2),
                (P.STf[0:1, :], P.i_trow), (P.SR[0:1, 120:128], P.i_scals),
                (P.XT[0:49, 0, :], P.i_vecs_a),
                (P.XT[0:8, 1, :], P.i_gate_w),
            ]:
                E.op("dma_start", out=dst, in_=srcap).inc(qL, 16)
            M.mark("L_consts", qL)
        if nm == "gpsimd":
            casts = [
                ("w_in", P.s_w_in, P.i_w_in),
                ("w_delta", P.s_w_delta, P.i_w_delta),
                ("w_B", P.s_w_B, P.i_w_B),
                ("w_C", P.s_w_C, P.i_w_C),
                ("w_out", P.s_w_out, P.i_w_out),
                ("qkv_0", P.s_qkv[:, 0:128, :],
                 P.i_qkv_w[0:1024].rearrange("(hp r) c -> hp r c", hp=8)),
                ("qkv_1", P.s_qkv[:, 128:256, :],
                 P.i_qkv_w[1024:2048].rearrange("(hp r) c -> hp r c", hp=8)),
                ("qkv_2", P.s_qkv[:, 256:384, :],
                 P.i_qkv_w[2048:3072].rearrange("(hp r) c -> hp r c", hp=8)),
                ("out_w", P.s_out_w, P.i_out_w),
                ("b_out_m", P.BR[0:1, 0, :], P.i_b_out_m.unsqueeze(0)),
                ("out_b_f", P.BR[0:1, 1, :], P.i_out_b_f.unsqueeze(0)),
            ]
            for cn, dst, srcap in casts:
                P.qg_dma(E, dst, srcap, markname=f"cast_{cn}", depth=2)
        if nm == "scalar":
            E.wait("L_consts")
            E.wait("v_p0_memset")
            E.op("activation", P.SR[0:1, 0:1], P.SR[0:1, 120:121], AF.Ln).inc(aS, 1)
            M.mark("a_ln4", aS)
            E.op("activation", P.SR[0:1, 2:3], P.SR[0:1, 120:121], AF.Arctan,
                 scale=0.0, bias=P.SV[0:1, 0, 53:54]).inc(aS, 1)
            M.mark("a_pi4", aS)
            E.wait("v_sccols")
            E.op("activation", P.SC[:, 0:1], P.jcol[:], AF.Exp, scale=P.SC[:, 1:2]).inc(aS, 1)
            M.mark("a_invf", aS)
            E.wait("v_red_sin")
            E.op("activation", P.RT[:, 1, :], P.STf, AF.Sin).inc(aS, 1)
            M.mark("a_sin", aS)
            E.wait("v_red_cos")
            E.op("activation", P.RT[:, 0, :], P.DEC[:], AF.Sin).inc(aS, 1)
            M.mark("a_tables", aS)
        if nm == "tensor":
            E.wait("L_consts")
            for c in range(NCC):
                if c >= 2:
                    E.wait(f"v_svT_{c - 2}")
                E.op("transpose", P.bank(c % 2)[:, 0:49],
                     P.XT[0:49, 0, 128 * c:128 * (c + 1)], P.identf[0:49, 0:49]).inc(tS, 1)
                M.mark(f"t_svT_{c}", tS)
            for c in range(NCC):
                if c >= 2:
                    E.wait(f"v_gwT_{c - 2}")
                E.op("transpose", P.bank(2 + c % 2)[:, 0:8],
                     P.XT[0:8, 1, 128 * c:128 * (c + 1)], P.identf[0:8, 0:8]).inc(tS, 1)
                M.mark(f"t_gwT_{c}", tS)
            for j, r in enumerate(ROWS):
                if j >= 1:
                    E.wait(f"v_rowe_{j - 1}")
                h = 1024 * (j % 2)
                E.op("matmul", P.bank(4)[0:1, 0:512], P.identf[0:49, r:r + 1],
                     P.XT[0:49, 0, 0:512], start=True, stop=True)
                E.op("matmul", P.bank(5)[0:1, 0:512], P.identf[0:49, r:r + 1],
                     P.XT[0:49, 0, 512:1024], start=True, stop=True).inc(tS, 1)
                M.mark(f"t_row_{j}", tS)
                if j < 3:
                    E.wait(f"v_rowe_{j}")
                    if j >= 1:
                        E.wait(f"v_bc_{j - 1}")
                    E.op("matmul", P.bank(6)[:, :], P.onesrf[0:1, :],
                         P.YS[0:1, h:h + 512], start=True, stop=True)
                    E.op("matmul", P.bank(7)[:, :], P.onesrf[0:1, :],
                         P.YS[0:1, h + 512:h + 1024], start=True, stop=True).inc(tS, 1)
                    M.mark(f"t_bc_{j}", tS)
            E.wait("v_sc_ready")
            for jj, cell in enumerate([1, 4, 5, 6, 7]):
                E.op("matmul", P.bank(4)[:, jj:jj + 1], P.onesrf[0:1, :],
                     P.SR[0:1, cell:cell + 1], start=True, stop=True).inc(tS, 1)
            M.mark("t_sccols", tS)
            E.wait("a_invf")
            E.op("transpose", P.bank(5)[0:1, 0:128], P.SC[:, 0:1], P.identf[:]).inc(tS, 1)
            M.mark("t_invr", tS)
            E.wait("v_invr")
            for n in range(4):
                E.op("matmul", P.bank(0 + n)[:, :], P.SR[0:1, 0:128],
                     P.STf[0:1, 512 * n:512 * (n + 1)], start=True, stop=True).inc(tS, 1)
            M.mark("t_freqs", tS)
            E.wait("v_red_cos")
            E.wait(f"v_rowe_3")
            E.op("matmul", P.bank(2)[:, :], P.onesrf[0:1, :],
                 P.YS[0:1, 1024:1536], start=True, stop=True)
            E.op("matmul", P.bank(3)[:, :], P.onesrf[0:1, :],
                 P.YS[0:1, 1536:2048], start=True, stop=True).inc(tS, 1)
            M.mark("t_bc_3", tS)
        if nm == "vector":
            E.op("memset", P.SV[:, 0, 51:52], 1e-5)
            E.op("memset", P.SV[:, 0, 52:53], 1e-6)
            E.op("memset", P.SV[:, 0, 53:54], 1.0).inc(vS, 1)
            M.mark("v_p0_memset", vS)
            for c in range(NCC):
                E.wait(f"t_svT_{c}")
                E.op("tensor_copy", P.SV[:, c, 0:49], P.bank(c % 2)[:, 0:49]).inc(vS, 1)
                M.mark(f"v_svT_{c}", vS)
            for c in range(NCC):
                E.wait(f"t_gwT_{c}")
                E.op("tensor_copy", P.GW[:, c, :], P.bank(2 + c % 2)[:, 0:8]).inc(vS, 1)
                M.mark(f"v_gwT_{c}", vS)
            E.op("tensor_scalar_mul", P.SV[:, :, 49:50], P.SV[:, :, 3:4], -1.0)
            E.op("tensor_scalar_mul", P.SV[:, :, 50:51], P.SV[:, :, 11:12], 0.125).inc(vS, 1)
            M.mark("v_svd", vS)
            for j in range(4):
                h = 1024 * (j % 2)
                E.wait(f"t_row_{j}")
                E.op("tensor_copy", P.YS[0:1, h:h + 512], P.PS[0:1, 4 * 512:4 * 512 + 512])
                E.op("tensor_copy", P.YS[0:1, h + 512:h + 1024],
                     P.PS[0:1, 5 * 512:5 * 512 + 512]).inc(vS, 1)
                M.mark(f"v_rowe_{j}", vS)
                if j < 2:
                    E.wait(f"t_bc_{j}")
                    E.op("tensor_copy", P.NRM[:, j, 0:512], P.bank(6))
                    E.op("tensor_copy", P.NRM[:, j, 512:1024], P.bank(7)).inc(vS, 1)
                    M.mark(f"v_bc_{j}", vS)
            E.wait("a_ln4")
            E.op("tensor_scalar_mul", P.SR[0:1, 1:2], P.SR[0:1, 0:1], -0.03125).inc(vS, 1)
            E.wait("a_pi4")
            E.op("tensor_scalar_mul", P.SR[0:1, 3:4], P.SR[0:1, 2:3], 8.0)
            E.op("reciprocal", P.SR[0:1, 4:5], P.SR[0:1, 3:4])
            E.op("tensor_scalar_mul", P.SR[0:1, 5:6], P.SR[0:1, 3:4], -1.0)
            E.op("tensor_scalar_mul", P.SR[0:1, 6:7], P.SR[0:1, 2:3], 2.0)
            E.op("tensor_scalar_mul", P.SR[0:1, 7:8], P.SR[0:1, 2:3], 4.0).inc(vS, 1)
            M.mark("v_sc_ready", vS)
            E.wait("t_sccols")
            E.op("tensor_copy", P.SC[:, 1:2], P.bank(4)[:, 0:1])
            E.op("tensor_copy", P.SC[:, 2:3], P.bank(4)[:, 1:2])
            E.op("tensor_copy", P.SC[:, 3:4], P.bank(4)[:, 2:3])
            E.op("tensor_copy", P.SC[:, 4:5], P.bank(4)[:, 3:4])
            E.op("tensor_copy", P.SC[:, 5:6], P.bank(4)[:, 4:5]).inc(vS, 1)
            M.mark("v_sccols", vS)
            E.wait("t_invr")
            E.op("tensor_copy", P.SR[0:1, 0:128], P.bank(5)[0:1, 0:128]).inc(vS, 1)
            M.mark("v_invr", vS)
            E.wait("t_freqs")
            FR = P.PS[:, 0:2048]
            KI = P.XTf.bitcast(I32)
            E.op("tensor_scalar", P.DEC[:], FR, P.SC[:, 2:3], None, ALU.mult)
            E.op("tensor_copy", KI, P.DEC[:])
            E.op("tensor_copy", P.DEC[:], KI)
            E.op("scalar_tensor_tensor", P.STf, P.DEC[:], P.SC[:, 3:4], FR,
                 ALU.mult, ALU.add)
            E.op("tensor_scalar", P.XTf, P.STf, P.SC[:, 5:6], None, ALU.is_gt)
            E.op("scalar_tensor_tensor", P.STf, P.XTf, P.SC[:, 3:4], P.STf,
                 ALU.mult, ALU.add).inc(vS, 1)
            M.mark("v_red_sin", vS)
            E.wait("a_sin")
            E.op("tensor_scalar", P.STf, FR, P.SC[:, 4:5], None, ALU.add)
            E.op("tensor_scalar", P.DEC[:], P.STf, P.SC[:, 2:3], None, ALU.mult)
            E.op("tensor_copy", KI, P.DEC[:])
            E.op("tensor_copy", P.DEC[:], KI)
            E.op("scalar_tensor_tensor", P.DEC[:], P.DEC[:], P.SC[:, 3:4], P.STf,
                 ALU.mult, ALU.add)
            E.op("tensor_scalar", P.XTf, P.DEC[:], P.SC[:, 5:6], None, ALU.is_gt)
            E.op("scalar_tensor_tensor", P.DEC[:], P.XTf, P.SC[:, 3:4], P.DEC[:],
                 ALU.mult, ALU.add).inc(vS, 1)
            M.mark("v_red_cos", vS)
            E.wait("a_tables")
            E.wait("t_bc_2")
            E.op("tensor_copy", P.DEC[:, 0:512], P.bank(6))
            E.op("tensor_copy", P.DEC[:, 512:1024], P.bank(7)).inc(vS, 1)
            E.wait("t_bc_3")
            E.op("tensor_copy", P.DEC[:, 1024:1536], P.bank(2))
            E.op("tensor_copy", P.DEC[:, 1536:2048], P.bank(3)).inc(vS, 1)
            M.mark("v_lnglnb", vS)

    def pA(self, E):
        P, nm = self, E.name
        qL, qA, qG, tS, vS, aS, pS = "qL", "qA", "qG", "tS", "vS", "aS", "pS"
        M = self.M
        OneK = 1.0 / 1024.0

        if nm == "sync":
            for i in range(NTT):
                if i == 0:
                    E.wait("v_red_cos")
                    E.wait("t_bc_3")
                if i >= 2:
                    E.wait(f"a1T_{i - 2}")
                    E.wait(f"a1xns_{i - 2}")
                E.op("dma_start", out=P.XT[:, i % 2, :],
                     in_=P.i_x[128 * i:128 * (i + 1), :]).inc(qL, 16)
                M.mark(f"a1x_{i}", qL)
                E.wait(f"a1xn_{i}")
                E.op("dma_start", out=P.s_xn[128 * i:128 * (i + 1), :],
                     in_=P.XT[:, i % 2, :]).inc(qL, 16)
                M.mark(f"a1xns_{i}", qL)
        if nm == "vector":
            for i in range(NTT):
                E.wait(f"a1x_{i}")
                xt = P.XT[:, i % 2, :]
                C = P.STC
                E.op("reduce_sum", C[:, 32:33], xt, axis=AX.X)
                E.wait(f"a1sq_{i}")
                E.op("tensor_scalar_mul", C[:, 34:35], C[:, 32:33], OneK)
                E.op("tensor_mul", C[:, 35:36], C[:, 34:35], C[:, 34:35])
                E.op("scalar_tensor_tensor", C[:, 36:37], C[:, 33:34], OneK,
                     C[:, 35:36], ALU.mult, ALU.subtract).inc(vS, 1)
                M.mark(f"a1var_{i}", vS)
                E.wait(f"a1rstd_{i}")
                E.op("tensor_scalar", xt, xt, C[:, 34:35], C[:, 37:38],
                     ALU.subtract, ALU.mult).inc(vS, 1)
                M.mark(f"a1xn_{i}", vS)
                E.op("reduce_sum", C[:, 38:39], xt, axis=AX.X)
                E.wait(f"a1sq2_{i}")
                if i >= 1:
                    E.wait(f"a1m_{i - 1}")
                E.op("tensor_scalar_mul", C[:, 39:40], C[:, 38:39], OneK)
                E.op("tensor_mul", C[:, 40:41], C[:, 39:40], C[:, 39:40])
                E.op("scalar_tensor_tensor", C[:, 41:42], C[:, 33:34], OneK,
                     C[:, 40:41], ALU.mult, ALU.subtract).inc(vS, 1)
                M.mark(f"a1var2_{i}", vS)
                E.wait(f"a1rstd2_{i}")
                if i >= 2:
                    E.wait(f"a1T_{i - 2}")
                u = P.UST[:, i % 2, 0:1024]
                E.op("tensor_scalar", u, xt, C[:, 39:40], C[:, 42:43],
                     ALU.subtract, ALU.mult)
                E.op("tensor_mul", u, u, P.DEC[:, 0:1024])
                E.op("tensor_add", u, u, P.DEC[:, 1024:2048]).inc(vS, 1)
                M.mark(f"a1m_{i}", vS)
                if i >= 1:
                    E.wait(f"a1T_{i - 1}")
                    b = 4 + (i - 1) % 2
                    E.op("tensor_copy",
                         P.FM4[:, :, 128 * (i - 1):128 * i],
                         P.bank(b).bitcast(BF16).rearrange("p (c t) -> p c t", c=8)).inc(vS, 1)
                    M.mark(f"a1F_{i - 1}", vS)
            E.wait(f"a1T_{NTT - 1}")
            E.op("tensor_copy", P.FM4[:, :, 128 * (NTT - 1):128 * NTT],
                 P.bank(4 + (NTT - 1) % 2).bitcast(BF16).rearrange("p (c t) -> p c t", c=8)).inc(vS, 1)
            M.mark(f"a1F_{NTT - 1}", vS)
        if nm == "scalar":
            for i in range(NTT):
                E.wait(f"a1x_{i}")
                if i >= 1:
                    E.wait(f"a1var2_{i - 1}")
                xt = P.XT[:, i % 2, :]
                C = P.STC
                E.op("activation", P.DUMP[:], xt, AF.Square,
                     accum_out=C[:, 33:34]).inc(aS, 1)
                M.mark(f"a1sq_{i}", aS)
                E.wait(f"a1var_{i}")
                E.op("activation", C[:, 37:38], C[:, 36:37], AF.Ln,
                     bias=P.SV[:, 0, 51:52])
                E.op("activation", C[:, 37:38], C[:, 37:38], AF.Exp,
                     scale=-0.5).inc(aS, 1)
                M.mark(f"a1rstd_{i}", aS)
                E.wait(f"a1xn_{i}")
                E.op("activation", P.DUMP[:], xt, AF.Square,
                     accum_out=C[:, 33:34]).inc(aS, 1)
                M.mark(f"a1sq2_{i}", aS)
                E.wait(f"a1var2_{i}")
                if i >= 1:
                    E.wait(f"a1m_{i - 1}")
                E.op("activation", C[:, 42:43], C[:, 41:42], AF.Ln,
                     bias=P.SV[:, 0, 51:52])
                E.op("activation", C[:, 42:43], C[:, 42:43], AF.Exp,
                     scale=-0.5).inc(aS, 1)
                M.mark(f"a1rstd2_{i}", aS)
                if i < 4:
                    if i == 0:
                        E.wait("cast_w_in")
                    for c in (2 * i, 2 * i + 1):
                        E.op("dma_start", out=P.slaV[:, c, :],
                             in_=P.s_w_in[:, 128 * c:128 * (c + 1)],
                             transpose=True).inc(qA, 16)
                    if i == 3:
                        M.mark("L_winT", qA)
                elif 6 <= i < 10:
                    if i == 6:
                        E.wait("cast_w_delta")
                    for c in (2 * (i - 6), 2 * (i - 6) + 1):
                        E.op("dma_start", out=P.SLB[:, c, :],
                             in_=P.s_w_delta[:, 128 * c:128 * (c + 1)],
                             transpose=True).inc(qA, 16)
                elif 10 <= i < 14:
                    if i == 10:
                        E.wait("cast_w_B")
                    for c in (2 * (i - 10), 2 * (i - 10) + 1):
                        E.op("dma_start", out=P.SLB[:, 8 + c, :],
                             in_=P.s_w_B[:, 128 * c:128 * (c + 1)],
                             transpose=True).inc(qA, 16)
                    if i == 13:
                        M.mark("L_wdB", qA)
        if nm == "tensor":
            for i in range(NTT):
                E.wait(f"a1m_{i}")
                if i >= 2:
                    E.wait(f"a1F_{i - 2}")
                b = 4 + i % 2
                for c in range(NCC):
                    ap = P.bank(b).bitcast(BF16)[:, 128 * c:128 * (c + 1)]
                    E.op("transpose", ap, P.UST[:, i % 2, 128 * c:128 * (c + 1)],
                         P.identb[:]).inc(tS, 1 if c == NCC - 1 else 0)
                M.mark(f"a1T_{i}", tS)

        ust2 = lambda o: (P.HC if o % 2 == 0 else P.PT2)[:, :]
        if nm == "tensor":
            E.wait("L_winT")
            for o in range(16):
                for n in range(4):
                    g = o * 4 + n
                    b = g % 4
                    if o == 0:
                        E.wait(f"a1F_{4 * n + 3}")
                    if g >= 4:
                        E.wait(f"a2e_{g - 4}")
                    for ic in range(NCC):
                        E.op("matmul", P.bank(b), P.slaV[:, ic, 128 * o:128 * (o + 1)],
                             P.FM4[:, ic, 512 * n:512 * (n + 1)],
                             start=(ic == 0), stop=(ic == NCC - 1)).inc(
                                 tS, 1 if ic == NCC - 1 else 0)
                    M.mark(f"a2g_{g}", tS)
        if nm == "scalar":
            for o in range(16):
                if o >= 2:
                    if o - 2 < 8:
                        E.wait(f"a2c_{o - 2}")
                    else:
                        E.wait(f"a2vs_{o - 2}")
                for n in range(4):
                    g = o * 4 + n
                    E.wait(f"a2g_{g}")
                    dst = ust2(o)[:, 512 * n:512 * (n + 1)]
                    if o < 8:
                        E.op("activation", dst, P.bank(g % 4), AF.Identity,
                             bias=P.sv(o, 8)).inc(aS, 1)
                    else:
                        E.op("activation", dst, P.bank(g % 4), AF.Sigmoid,
                             bias=P.sv(o - 8, 9)).inc(aS, 1)
                    M.mark(f"a2e_{g}", aS)
                M.mark(f"a2eo_{o}", aS)
        if nm == "vector":
            for o in range(8):
                E.wait(f"a2eo_{o}")
                uc = P.biga8[:, o, :]
                u = ust2(o)
                E.op("tensor_scalar", uc, u, P.sv(o, 48), P.sv(o, 2),
                     ALU.mult, ALU.add)
                E.op("scalar_tensor_tensor", uc[:, 1:T], u[:, 0:T - 1], P.sv(o, 47),
                     uc[:, 1:T], ALU.mult, ALU.add)
                E.op("scalar_tensor_tensor", uc[:, 2:T], u[:, 0:T - 2], P.sv(o, 46),
                     uc[:, 2:T], ALU.mult, ALU.add)
                E.op("scalar_tensor_tensor", uc[:, 3:T], u[:, 0:T - 3], P.sv(o, 45),
                     uc[:, 3:T], ALU.mult, ALU.add).inc(vS, 1)
                M.mark(f"a2c_{o}", vS)
        if nm == "sync":
            for o in range(8, 16):
                E.wait(f"a2eo_{o}")
                E.op("dma_start", out=P.s_sgv[128 * (o - 8):128 * (o - 7), :],
                     in_=ust2(o)).inc(qL, 16)
                M.mark(f"a2vs_{o}", qL)
            E.wait("cast_w_out")
            E.wait("a2g_63")
            for c in range(NCC):
                E.op("dma_start", out=P.SLA[:, c, :],
                     in_=P.s_w_C[:, 128 * c:128 * (c + 1)], transpose=True).inc(qL, 16)
            for c in range(NCC):
                E.op("dma_start", out=P.SLA[:, 8 + c, :],
                     in_=P.s_w_out[:, 128 * c:128 * (c + 1)], transpose=True).inc(qL, 16)
            M.mark("L_wCo", qL)

        if nm == "tensor":
            E.wait("L_wdB")
            for s in range(NCC):
                if s == 0:
                    E.wait("a2eo_15")
                if s >= 1:
                    E.wait(f"a3hc_{s - 1}")
                for n in range(4):
                    for ic in range(NCC):
                        E.op("matmul", P.bank(n), P.SLB[:, ic, 128 * s:128 * (s + 1)],
                             P.biga8[:, ic, 512 * n:512 * (n + 1)],
                             start=(ic == 0), stop=(ic == NCC - 1)).inc(
                                 tS, 1 if ic == NCC - 1 else 0)
                M.mark(f"a3z_{s}", tS)
                if s >= 1:
                    E.wait(f"a3ys_{s - 1}")
                for n in range(4):
                    for ic in range(NCC):
                        E.op("matmul", P.bank(4 + n), P.SLB[:, 8 + ic, 128 * s:128 * (s + 1)],
                             P.biga8[:, ic, 512 * n:512 * (n + 1)],
                             start=(ic == 0), stop=(ic == NCC - 1)).inc(
                                 tS, 1 if ic == NCC - 1 else 0)
                M.mark(f"a3b_{s}", tS)
                E.wait(f"a3d_{s}")
                if s == 0:
                    E.wait("L_wCo")
                for n in range(4):
                    for ic in range(NCC):
                        E.op("matmul", P.bank(n), P.SLA[:, ic, 128 * s:128 * (s + 1)],
                             P.biga8[:, ic, 512 * n:512 * (n + 1)],
                             start=(ic == 0), stop=(ic == NCC - 1)).inc(
                                 tS, 1 if ic == NCC - 1 else 0)
                M.mark(f"a3c_{s}", tS)
                E.wait(f"a3hc_{s}")
                E.wait(f"a3i_{s}")
                for n in range(4):
                    E.op("matmul", P.bank(4 + n)[0:1, :], P.onesc[:],
                         P.HC[:, 512 * n:512 * (n + 1)], start=True, stop=True).inc(
                             tS, 1 if n == 3 else 0)
                M.mark(f"a3y_{s}", tS)
        if nm == "scalar":
            for s in range(NCC):
                E.wait(f"a3z_{s}")
                E.wait(f"a3dc_{s - 1}") if s >= 1 else None
                E.op("activation", P.DEC[:], P.PS[:, 0:2048], AF.Sigmoid,
                     scale=-1.0, bias=P.sv(s, 49)).inc(aS, 1)
                M.mark(f"a3d_{s}", aS)
        if nm == "vector":
            for s in range(NCC):
                E.wait(f"a3b_{s}")
                E.op("scalar_tensor_tensor", P.INP[:], P.PS[:, 2048:4096], P.sv(s, 4),
                     P.biga8[:, s, :], ALU.add, ALU.mult).inc(vS, 1)
                M.mark(f"a3i_{s}", vS)
                E.wait(f"a3d_{s}")
                E.op("tensor_tensor_scan", P.HS[:], P.DEC[:], P.INP[:], 0.0,
                     ALU.mult, ALU.add).inc(vS, 1)
                M.mark(f"a3dc_{s}", vS)
                E.wait(f"a3c_{s}")
                if s == 0:
                    E.wait("a2vs_14")
                E.op("scalar_tensor_tensor", P.HC[:], P.PS[:, 0:2048], P.sv(s, 5),
                     P.HS[:], ALU.add, ALU.mult).inc(vS, 1)
                M.mark(f"a3hc_{s}", vS)
                E.wait(f"a3y_{s}")
                if s == 0:
                    E.op("tensor_copy", P.YS[0:1, :], P.PS[0:1, 2048:4096]).inc(vS, 1)
                else:
                    E.op("tensor_add", P.YS[0:1, :], P.YS[0:1, :],
                         P.PS[0:1, 2048:4096]).inc(vS, 1)
                M.mark(f"a3ys_{s}", vS)

        if nm == "vector":
            E.op("tensor_copy", P.RB[0:1, :], P.YS[0:1, :]).inc(vS, 1)
            M.mark("a4rb", vS)
        if nm == "tensor":
            E.wait("a4rb")
            E.wait(f"a3hc_{NCC - 1}")
            for n in range(4):
                E.op("matmul", P.bank(n), P.onesr[0:1, :],
                     P.RB[0:1, 512 * n:512 * (n + 1)], start=True, stop=True).inc(
                         tS, 1 if n == 3 else 0)
            M.mark("a4bc", tS)
        if nm == "sync":
            for s in range(NCC):
                if s == 0:
                    E.wait("a2vs_15")
                    E.wait(f"a3y_{NCC - 1}")
                if s == 1:
                    E.wait("a2c_7")
                if s >= 2:
                    E.wait(f"a4y_{s - 2}")
                buf = P.HC if s % 2 == 0 else P.PT2
                E.op("dma_start", out=buf[:],
                     in_=P.s_sgv[128 * s:128 * (s + 1), :]).inc(qL, 16)
                M.mark(f"a4g_{s}", qL)
        if nm == "vector":
            for s in range(NCC):
                E.wait(f"a4g_{s}")
                if s == 0:
                    E.wait("a4bc")
                buf = P.HC if s % 2 == 0 else P.PT2
                E.op("tensor_mul", P.biga8[:, s, :], P.PS[:, 0:2048], buf[:]).inc(vS, 1)
                M.mark(f"a4y_{s}", vS)
        if nm == "sync":
            for i in range(NTT):
                if i == 0:
                    E.wait(f"a1xns_{NTT - 1}")
                if i >= 2:
                    E.wait(f"a4s_{i - 2}")
                    E.wait(f"a4ss_{i - 2}")
                E.op("dma_start", out=P.XT[:, i % 2, :],
                     in_=P.s_xn[128 * i:128 * (i + 1), :]).inc(qL, 16)
                M.mark(f"a4x_{i}", qL)
                E.wait(f"a4s_{i}")
                E.op("dma_start", out=P.s_s[128 * i:128 * (i + 1), :],
                     in_=P.ST[:, i % 2, :]).inc(qL, 16)
                M.mark(f"a4ss_{i}", qL)
        if nm == "tensor":
            E.wait(f"a4y_{NCC - 1}")
            E.wait("cast_b_out_m")
            for i in range(NTT):
                bb = 4 + 2 * (i % 2)
                if i >= 2:
                    E.wait(f"a4s_{i - 2}")
                for n in range(2):
                    for s in range(NCC):
                        E.op("matmul", P.bank(bb + n),
                             P.biga8[:, s, 128 * i:128 * (i + 1)],
                             P.SLA[:, 8 + s, 512 * n:512 * (n + 1)],
                             start=(s == 0), stop=False)
                    E.op("matmul", P.bank(bb + n), P.onesr[0:1, 0:128],
                         P.BR[0:1, 0, 512 * n:512 * (n + 1)],
                         start=False, stop=True).inc(tS, 1)
                M.mark(f"a4o_{i}", tS)
        if nm == "vector":
            for i in range(NTT):
                E.wait(f"a4o_{i}")
                E.wait(f"a4x_{i}")
                bb = 4 + 2 * (i % 2)
                E.op("tensor_add", P.ST[:, i % 2, :], P.XT[:, i % 2, :],
                     P.PS[:, 512 * bb:512 * (bb + 2)]).inc(vS, 1)
                M.mark(f"a4s_{i}", vS)

    def pB(self, E):
        P, nm = self, E.name
        qL, qA, qG, tS, vS, aS, pS = "qL", "qA", "qG", "tS", "vS", "aS", "pS"
        M = self.M
        OneK = 1.0 / 1024.0
        QR = self.INP.bitcast(BF16)[:, 0:2048]
        KR = self.HS.bitcast(BF16)[:, 0:2048]

        if nm == "sync":
            for i in range(NTT):
                if i == 0:
                    E.wait(f"a4s_{NTT - 2}")
                    E.wait(f"a4s_{NTT - 1}")
                if i >= 2:
                    E.wait(f"b0T_{i - 2}")
                E.op("dma_start", out=P.XT[:, i % 2, :],
                     in_=P.s_xn[128 * i:128 * (i + 1), :]).inc(qL, 16)
                M.mark(f"b0x_{i}", qL)
            E.wait("cast_out_w")
            E.wait(f"a4o_{NTT - 1}")
            for c in range(NCC):
                E.op("dma_start", out=P.SLA[:, c, :],
                     in_=P.s_out_w[:, 128 * c:128 * (c + 1)], transpose=True).inc(qL, 16)
            M.mark("L_owT", qL)
        if nm == "scalar":
            for i in range(NTT):
                E.wait(f"b0x_{i}")
                C = P.STC
                E.op("activation", P.DUMP[:], P.XT[:, i % 2, :], AF.Square,
                     accum_out=C[:, 44:45]).inc(aS, 1)
                if i >= 1:
                    E.wait(f"b0m_{i - 1}")
                E.op("activation", C[:, 45:46], C[:, 44:45], AF.Ln,
                     scale=OneK, bias=P.SV[:, 0, 52:53])
                E.op("activation", C[:, 45:46], C[:, 45:46], AF.Exp,
                     scale=-0.5).inc(aS, 1)
                M.mark(f"b0r_{i}", aS)
        if nm == "vector":
            for i in range(NTT):
                E.wait(f"b0r_{i}")
                if i >= 2:
                    E.wait(f"b0T_{i - 2}")
                u = P.UST[:, i % 2, 0:1024]
                E.op("tensor_scalar", u, P.XT[:, i % 2, :], P.STC[:, 45:46], None,
                     ALU.mult)
                E.op("tensor_mul", u, u, P.NRM[:, 0, :]).inc(vS, 1)
                M.mark(f"b0m_{i}", vS)
                if i >= 1:
                    E.wait(f"b0T_{i - 1}")
                    b = (i - 1) % 2
                    E.op("tensor_copy",
                         P.FM4[:, :, 128 * (i - 1):128 * i],
                         P.bank(b).bitcast(BF16).rearrange("p (c t) -> p c t", c=8)).inc(vS, 1)
                    M.mark(f"b0F_{i - 1}", vS)
            E.wait(f"b0T_{NTT - 1}")
            E.op("tensor_copy", P.FM4[:, :, 128 * (NTT - 1):128 * NTT],
                 P.bank((NTT - 1) % 2).bitcast(BF16).rearrange("p (c t) -> p c t", c=8)).inc(vS, 1)
            M.mark(f"b0F_{NTT - 1}", vS)
        if nm == "tensor":
            for i in range(NTT):
                E.wait(f"b0m_{i}")
                if i >= 2:
                    E.wait(f"b0F_{i - 2}")
                b = i % 2
                for c in range(NCC):
                    ap = P.bank(b).bitcast(BF16)[:, 128 * c:128 * (c + 1)]
                    E.op("transpose", ap, P.UST[:, i % 2, 128 * c:128 * (c + 1)],
                         P.identb[:]).inc(tS, 1 if c == NCC - 1 else 0)
                M.mark(f"b0T_{i}", tS)

        def _qkv_prefetch(hp):
            for ic in range(NCC):
                E.op("dma_start", out=P.QKT[:, hp % 2, ic, :],
                     in_=P.s_qkv[hp, :, 128 * ic:128 * (ic + 1)],
                     transpose=True).inc(qL, 16)
            M.mark(f"L_qkv_{hp}", qL)
        if nm == "vector":
            E.wait("a3b_7")
            E.op("memset", P.VPs[0][:, :, 64:65], 1.0)
            E.op("memset", P.VPs[0][:, :, 129:130], 1.0)
            E.op("memset", P.VPs[1][:, :, 64:65], 1.0)
            E.op("memset", P.VPs[1][:, :, 129:130], 1.0)
            E.op("memset", P.KA[64:65, :], -1.0).inc(vS, 1)
            M.mark("b1vpones", vS)
        for hp in range(8):
            sl = lambda h=None: P.QKT[:, hp % 2, :, :]
            sln = P.QKT[:, (hp + 1) % 2, :, :]
            VP = P.VPs[hp % 2]
            VPn = P.VPs[(hp + 1) % 2]
            if nm == "sync":
                if hp == 0:
                    E.wait("cast_qkv_2")
                    E.wait("a3b_7")
                    _qkv_prefetch(0)
                    _qkv_prefetch(1)
                elif hp >= 2:
                    E.wait(f"b1qk_{hp - 2}")
                    E.wait(f"b1k_{hp - 2}")
                    _qkv_prefetch(hp)
            if nm == "gpsimd":
                for h2 in range(2):
                    hb = 64 * h2
                    E.wait(f"b1ropek_{hp}")
                    if h2 == 1:
                        E.wait(f"b1st_{hp}_0_{NTT - 1}_1")
                    P.qg_dma(E, P.QA[0:64, :], QR[hb:hb + 64, :], depth=4)
                    P.qg_dma(E, P.KA[0:64, :], KR[hb:hb + 64, :], depth=4)
                    E.wait(f"b1mrow_{hp}_{h2}")
                    P.qg_dma(E, P.QA[64:65, :], P.RB[0:1, :],
                             markname=f"b1qa_{hp}_{h2}", depth=4)
                P.qg_dma(E, P.s_w1[hp], P.i_w1[hp], markname=f"cast_w1_{hp}",
                         depth=2)
                P.qg_dma(E, P.s_w2[hp], P.i_w2[hp], markname=f"cast_w2_{hp}",
                         depth=2)
                P.qg_dma(E, P.s_w3[hp], P.i_w3[hp], markname=f"cast_w3_{hp}",
                         depth=2)
            if nm == "tensor":
                E.wait(f"L_qkv_{hp}")
                if hp == 0:
                    E.wait(f"b0F_{NTT - 1}")
                else:
                    E.wait(f"b1rle_{hp - 1}_1")
                for n in range(4):
                    for ic in range(NCC):
                        E.op("matmul", P.bank(n), sl()[:, ic, 0:128],
                             P.FM4[:, ic, 512 * n:512 * (n + 1)],
                             start=(ic == 0), stop=(ic == NCC - 1)).inc(
                                 tS, 1 if ic == NCC - 1 else 0)
                M.mark(f"b1q_{hp}", tS)
                if hp >= 1:
                    E.wait(f"b1ao_{hp - 1}_1")
                for n in range(4):
                    for ic in range(NCC):
                        E.op("matmul", P.bank(4 + n), sl()[:, ic, 128:256],
                             P.FM4[:, ic, 512 * n:512 * (n + 1)],
                             start=(ic == 0), stop=(ic == NCC - 1)).inc(
                                 tS, 1 if ic == NCC - 1 else 0)
                M.mark(f"b1k_{hp}", tS)
                E.wait(f"b1qe_{hp}")
                for n in range(4):
                    E.op("matmul", P.bank(n), P.permT[:],
                         QR[:, 512 * n:512 * (n + 1)], start=True, stop=True).inc(
                             tS, 1 if n == 3 else 0)
                M.mark(f"b1rq_{hp}", tS)
                E.wait(f"b1ke_{hp}")
                for n in range(4):
                    E.op("matmul", P.bank(4 + n), P.permT[:],
                         KR[:, 512 * n:512 * (n + 1)], start=True, stop=True).inc(
                             tS, 1 if n == 3 else 0)
                M.mark(f"b1rk_{hp}", tS)
                if hp == 0:
                    E.wait(f"b1ropeq_{hp}")
                    for tt in range(NTT):
                        b = tt // 4
                        off = 128 * (tt % 4)
                        for ic in range(NCC):
                            E.op("matmul", P.bank(b)[:, off:off + 128],
                                 P.FM4[:, ic, 128 * tt:128 * (tt + 1)],
                                 sl()[:, ic, 256:384],
                                 start=(ic == 0), stop=(ic == NCC - 1)).inc(
                                     tS, 1 if ic == NCC - 1 else 0)
                    M.mark(f"b1v_{hp}", tS)
                    M.mark(f"b1qk_{hp}", tS)
                hb = 0
                for qt in range(NTT):
                    grp = 4 * (qt % 2)
                    if qt == 0:
                        E.wait(f"b1ve_{hp}")
                        E.wait(f"b1ropeq_{hp}")
                    if qt == 1:
                        E.wait(f"b1ropek_{hp}")
                    if qt >= 2:
                        E.wait(f"b1mx_{hp}_0_{qt - 2}")
                    for n in range(4):
                        E.op("matmul", P.bank(grp + n),
                             QR[hb:hb + 64, 128 * qt:128 * (qt + 1)],
                             KR[hb:hb + 64, 512 * n:512 * (n + 1)],
                             start=True, stop=True).inc(tS, 1 if n == 3 else 0)
                    M.mark(f"b1s_{hp}_0_{qt}", tS)
                E.wait(f"b1cm_{hp}_0")
                E.op("transpose", P.bank(2)[0:16, 0:128],
                     P.MH[:, 0:16], P.identf[:]).inc(tS, 1)
                M.mark(f"b1mrT_{hp}_0", tS)
                E.wait(f"b1mre_{hp}_0")
                for qt in range(NTT):
                    bk = 4 + qt // 4
                    off = 128 * (qt % 4)
                    E.op("matmul", P.bank(bk)[0:1, off:off + 128],
                         P.sel16[:, qt:qt + 1], P.E16[0:16, 0:128],
                         start=True, stop=True).inc(tS, 1 if qt == NTT - 1 else 0)
                M.mark(f"b1mrr_{hp}_0", tS)
                for tk in range(NTT + 1):
                    if tk < NTT:
                        for qh in range(2):
                            bb = 2 * qh
                            if tk == 0:
                                E.wait(f"b1qa_{hp}_0")
                            else:
                                E.wait(f"b1exp_{hp}_0_{tk - 1}_{qh}")
                                E.wait(f"b1hx{qh}_{hp}_{tk - 1}")
                            for n in range(2):
                                E.op("matmul", P.bank(bb + n),
                                     P.KA[0:65, 128 * tk:128 * (tk + 1)],
                                     P.QA[0:65, 512 * (2 * qh + n):512 * (2 * qh + n + 1)],
                                     start=True, stop=True).inc(tS, 1 if n == 1 else 0)
                            M.mark(f"b1st_{hp}_0_{tk}_{qh}", tS)
                    if tk >= 1:
                        pk = tk - 1
                        ptb = P.HC if pk % 2 == 0 else P.PT2
                        for qh in range(2):
                            E.wait(f"b1exp_{hp}_0_{pk}_{qh}")
                            if pk == 0 and qh == 0 and hp == 0:
                                E.wait("b1vpones")
                            for n in range(2):
                                j = 2 * qh + n
                                E.op("matmul", P.bank(4 + j)[0:65, :],
                                     VP[:, pk, 0:65],
                                     ptb[:, 512 * j:512 * (j + 1)],
                                     start=(pk == 0), stop=(pk == NTT - 1),
                                     skip_group_check=True).inc(tS, 1 if n == 1 else 0)
                            M.mark(f"b1pv_{hp}_0_{pk}_{qh}", tS)
                    if tk < NTT:
                        for qh in range(2):
                            E.wait(f"b1exp_{hp}_0_{tk}_{qh}")
                            for n in range(2):
                                E.op("matmul", P.bank(2 * qh + n),
                                     QR[64:128, 128 * tk:128 * (tk + 1)],
                                     KR[64:128, 1024 * qh + 512 * n:1024 * qh + 512 * (n + 1)],
                                     start=True, stop=True).inc(tS, 1 if n == 1 else 0)
                            M.mark(f"b1s2{qh}_{hp}_{tk}", tS)
                E.wait(f"b1rs_{hp}_0")
                E.wait(f"b1hx0_{hp}_{NTT - 1}")
                E.wait(f"b1hx1_{hp}_{NTT - 1}")
                for n in range(4):
                    E.op("matmul", P.bank(n)[0:64, :], P.onesr[0:1, 0:64],
                         P.RB[0:1, 512 * n:512 * (n + 1)], start=True,
                         stop=True).inc(tS, 1 if n == 3 else 0)
                M.mark(f"b1rlb_{hp}_0", tS)
                E.wait(f"b1rle_{hp}_0")
                E.wait(f"b1cm_{hp}_1")
                E.op("transpose", P.bank(2)[0:16, 0:128],
                     P.MH[:, 0:16], P.identf[:]).inc(tS, 1)
                M.mark(f"b1mrT_{hp}_1", tS)
                E.wait(f"b1mre_{hp}_1")
                E.wait(f"b1ao_{hp}_0")
                for qt in range(NTT):
                    bk = 4 + qt // 4
                    off = 128 * (qt % 4)
                    E.op("matmul", P.bank(bk)[0:1, off:off + 128],
                         P.sel16[:, qt:qt + 1], P.E16[0:16, 0:128],
                         start=True, stop=True).inc(tS, 1 if qt == NTT - 1 else 0)
                M.mark(f"b1mrr_{hp}_1", tS)
                for tk in range(NTT + 1):
                    if tk < NTT:
                        for qh in range(2):
                            bb = 2 * qh
                            if tk == 0:
                                E.wait(f"b1qa_{hp}_1")
                            else:
                                E.wait(f"b1exp_{hp}_1_{tk - 1}_{qh}")
                                if qh == 0 and hp < 7:
                                    E.wait(f"b1vte_{hp + 1}_{tk - 1}")
                            for n in range(2):
                                E.op("matmul", P.bank(bb + n),
                                     P.KA[0:65, 128 * tk:128 * (tk + 1)],
                                     P.QA[0:65, 512 * (2 * qh + n):512 * (2 * qh + n + 1)],
                                     start=True, stop=True).inc(tS, 1 if n == 1 else 0)
                            M.mark(f"b1st_{hp}_1_{tk}_{qh}", tS)
                    if tk >= 1:
                        pk = tk - 1
                        ptb = P.HC if pk % 2 == 0 else P.PT2
                        for qh in range(2):
                            E.wait(f"b1exp_{hp}_1_{pk}_{qh}")
                            if pk == 0 and qh == 0:
                                E.wait(f"b1ao_{hp}_0")
                            for n in range(2):
                                j = 2 * qh + n
                                E.op("matmul", P.bank(4 + j)[0:65, :],
                                     VP[:, pk, 65:130],
                                     ptb[:, 512 * j:512 * (j + 1)],
                                     start=(pk == 0), stop=(pk == NTT - 1),
                                     skip_group_check=True).inc(tS, 1 if n == 1 else 0)
                            M.mark(f"b1pv_{hp}_1_{pk}_{qh}", tS)
                    if tk < NTT and hp < 7:
                        E.wait(f"b1exp_{hp}_1_{tk}_0")
                        if tk == 0:
                            E.wait(f"L_qkv_{hp + 1}")
                        for ic in range(NCC):
                            E.op("matmul", P.bank(0)[:, 0:128],
                                 P.FM4[:, ic, 128 * tk:128 * (tk + 1)],
                                 sln[:, ic, 256:384],
                                 start=(ic == 0), stop=(ic == NCC - 1),
                                 skip_group_check=True).inc(
                                     tS, 1 if ic == NCC - 1 else 0)
                        M.mark(f"b1vt_{hp + 1}_{tk}", tS)
                        if tk == NTT - 1:
                            M.mark(f"b1qk_{hp + 1}", tS)
                E.wait(f"b1rs_{hp}_1")
                if hp < 7:
                    E.wait(f"b1vte_{hp + 1}_{NTT - 1}")
                for n in range(4):
                    E.op("matmul", P.bank(n)[0:64, :], P.onesr[0:1, 0:64],
                         P.RB[0:1, 512 * n:512 * (n + 1)], start=True,
                         stop=True).inc(tS, 1 if n == 3 else 0)
                M.mark(f"b1rlb_{hp}_1", tS)
            if nm == "scalar":
                E.wait(f"b1q_{hp}")
                E.op("activation", QR, P.PS[:, 0:2048], AF.Identity,
                     bias=P.sv(hp, 10)).inc(aS, 1)
                M.mark(f"b1qe_{hp}", aS)
                E.wait(f"b1k_{hp}")
                E.op("activation", KR, P.PS[:, 2048:4096], AF.Identity,
                     scale=0.125, bias=P.sv(hp, 50)).inc(aS, 1)
                M.mark(f"b1ke_{hp}", aS)
                for h2 in range(2):
                    for tk in range(NTT):
                        ptb = P.HC if tk % 2 == 0 else P.PT2
                        for qh in range(2):
                            E.wait(f"b1st_{hp}_{h2}_{tk}_{qh}")
                            if tk >= 2:
                                E.wait(f"b1pv_{hp}_{h2}_{tk - 2}_{qh}")
                            E.op("activation", ptb[:, 1024 * qh:1024 * (qh + 1)],
                                 P.PS[:, 1024 * qh:1024 * (qh + 1)],
                                 AF.Exp).inc(aS, 1)
                            M.mark(f"b1exp_{hp}_{h2}_{tk}_{qh}", aS)
                    E.wait(f"b1pv_{hp}_{h2}_{NTT - 1}_1")
                    E.wait(f"b1qa_{hp}_{h2}")
                    E.op("activation", P.YS[0:1, :], P.PS[64:65, 2048:4096],
                         AF.Ln)
                    E.op("activation", P.RB[0:1, :], P.YS[0:1, :], AF.Exp,
                         scale=-1.0).inc(aS, 1)
                    M.mark(f"b1rs_{hp}_{h2}", aS)
            if nm == "vector":
                E.wait(f"b1rq_{hp}")
                E.op("tensor_mul", P.PT2[:], P.PS[:, 0:2048], P.RT[:, 1, :])
                E.op("tensor_mul", QR, QR, P.RT[:, 0, :])
                E.op("tensor_add", QR, QR, P.PT2[:]).inc(vS, 1)
                M.mark(f"b1ropeq_{hp}", vS)
                E.wait(f"b1rk_{hp}")
                E.op("tensor_mul", P.PT2[:], P.PS[:, 2048:4096], P.RT[:, 1, :])
                E.op("tensor_mul", KR, KR, P.RT[:, 0, :])
                E.op("tensor_add", KR, KR, P.PT2[:]).inc(vS, 1)
                M.mark(f"b1ropek_{hp}", vS)
                if hp == 0:
                    E.wait(f"b1v_{hp}")
                    for tt in range(NTT):
                        b = tt // 4
                        off = 128 * (tt % 4)
                        E.op("tensor_copy", VP[:, tt, 0:64],
                             P.bank(b)[:, off:off + 64])
                        E.op("tensor_copy", VP[:, tt, 65:129],
                             P.bank(b)[:, off + 64:off + 128]).inc(
                                 vS, 1 if tt == NTT - 1 else 0)
                    M.mark(f"b1ve_{hp}", vS)
                for qt in range(NTT):
                    E.wait(f"b1s_{hp}_0_{qt}")
                    E.op("reduce_max", P.MH[:, qt:qt + 1],
                         P.PS[:, 2048 * (qt % 2):2048 * (qt % 2) + 2048],
                         axis=AX.X).inc(vS, 1)
                    M.mark(f"b1mx_{hp}_0_{qt}", vS)
                M.mark(f"b1cm_{hp}_0", vS)
                E.wait(f"b1mrT_{hp}_0")
                E.op("tensor_copy", P.E16[0:16, 0:128],
                     P.bank(2)[0:16, 0:128]).inc(vS, 1)
                M.mark(f"b1mre_{hp}_0", vS)
                E.wait(f"b1mrr_{hp}_0")
                E.op("tensor_copy", P.RB[0:1, :], P.PS[0:1, 2048:4096]).inc(vS, 1)
                M.mark(f"b1mrow_{hp}_0", vS)
                for tk in range(NTT):
                    E.wait(f"b1s20_{hp}_{tk}")
                    E.wait(f"b1s21_{hp}_{tk}")
                    E.op("reduce_max", P.MH[:, tk:tk + 1],
                         P.PS[:, 0:2048], axis=AX.X).inc(vS, 1)
                    M.mark(f"b1hx0_{hp}_{tk}", vS)
                    M.mark(f"b1hx1_{hp}_{tk}", vS)
                M.mark(f"b1cm_{hp}_1", vS)
                E.wait(f"b1rlb_{hp}_0")
                E.op("tensor_copy", P.DEC[0:64, :], P.PS[0:64, 0:2048]).inc(vS, 1)
                M.mark(f"b1rle_{hp}_0", vS)
                dst = P.biga8[0:64, hp, :]
                E.op("tensor_mul", dst, P.PS[0:64, 2048:4096], P.DEC[0:64, :])
                E.op("tensor_scalar_add", dst, dst,
                     P.SV[0:64, hp, 12:13]).inc(vS, 1)
                M.mark(f"b1ao_{hp}_0", vS)
                E.wait(f"b1mrT_{hp}_1")
                E.op("tensor_copy", P.E16[0:16, 0:128],
                     P.bank(2)[0:16, 0:128]).inc(vS, 1)
                M.mark(f"b1mre_{hp}_1", vS)
                E.wait(f"b1mrr_{hp}_1")
                E.op("tensor_copy", P.RB[0:1, :], P.PS[0:1, 2048:4096]).inc(vS, 1)
                M.mark(f"b1mrow_{hp}_1", vS)
                if hp < 7:
                    for tk in range(NTT):
                        E.wait(f"b1vt_{hp + 1}_{tk}")
                        E.op("tensor_copy", VPn[:, tk, 0:64],
                             P.bank(0)[:, 0:64])
                        E.op("tensor_copy", VPn[:, tk, 65:129],
                             P.bank(0)[:, 64:128]).inc(vS, 1)
                        M.mark(f"b1vte_{hp + 1}_{tk}", vS)
                    M.mark(f"b1ve_{hp + 1}", vS)
                E.wait(f"b1rlb_{hp}_1")
                E.op("tensor_copy", P.DEC[0:64, :], P.PS[0:64, 0:2048]).inc(vS, 1)
                M.mark(f"b1rle_{hp}_1", vS)
                dst = P.biga8[64:128, hp, :]
                E.op("tensor_mul", dst, P.PS[0:64, 2048:4096], P.DEC[0:64, :])
                E.op("tensor_scalar_add", dst, dst,
                     P.SV[64:128, hp, 12:13]).inc(vS, 1)
                M.mark(f"b1ao_{hp}_1", vS)
                M.mark(f"b1ao2_{hp}", vS)
        if nm == "sync":
            for i in range(NTT):
                if i == 0:
                    E.wait(f"b0m_{NTT - 2}")
                    E.wait(f"b0m_{NTT - 1}")
                    E.wait(f"b0r_{NTT - 1}")
                if i >= 2:
                    E.wait(f"b3t_{i - 2}")
                    E.wait(f"b3ts_{i - 2}")
                E.op("dma_start", out=P.XT[:, i % 2, :],
                     in_=P.s_xn[128 * i:128 * (i + 1), :]).inc(qL, 16)
                M.mark(f"b3x_{i}", qL)
                E.wait(f"b3t_{i}")
                E.op("dma_start", out=P.s_t1[128 * i:128 * (i + 1), :],
                     in_=P.XT[:, i % 2, :]).inc(qL, 16)
                M.mark(f"b3ts_{i}", qL)
        if nm == "tensor":
            E.wait("L_owT")
            E.wait("b1ao2_7")
            E.wait("cast_out_b_f")
            for i in range(NTT):
                bb = 4 + 2 * (i % 2)
                if i >= 2:
                    E.wait(f"b3t_{i - 2}")
                for n in range(2):
                    for c in range(NCC):
                        E.op("matmul", P.bank(bb + n),
                             P.biga8[:, c, 128 * i:128 * (i + 1)],
                             P.SLA[:, c, 512 * n:512 * (n + 1)],
                             start=(c == 0), stop=False)
                    E.op("matmul", P.bank(bb + n), P.onesr[0:1, 0:128],
                         P.BR[0:1, 1, 512 * n:512 * (n + 1)],
                         start=False, stop=True).inc(tS, 1)
                M.mark(f"b3o_{i}", tS)
        if nm == "vector":
            for i in range(NTT):
                E.wait(f"b3o_{i}")
                E.wait(f"b3x_{i}")
                bb = 4 + 2 * (i % 2)
                E.op("tensor_add", P.XT[:, i % 2, :], P.XT[:, i % 2, :],
                     P.PS[:, 512 * bb:512 * (bb + 2)]).inc(vS, 1)
                M.mark(f"b3t_{i}", vS)

    def pC(self, E):
        P, nm = self, E.name
        qL, qA, qG, tS, vS, aS, pS = "qL", "qA", "qG", "tS", "vS", "aS", "pS"
        M = self.M
        OneK = 1.0 / 1024.0

        if nm == "sync":
            if "rt" in P.o_dbg:
                E.wait("a_tables")
                E.op("dma_start", out=P.o_dbg["rt"], in_=P.RT[:, :, :]).inc(qL, 16)
                M.mark("dbg_rt", qL)
            if "aon" in P.o_dbg:
                E.wait("b1ao2_7")
                E.op("dma_start", out=P.o_dbg["aon"], in_=P.BIGA[:, :]).inc(qL, 16)
                M.mark("dbg_aon", qL)
            if "t1" in P.o_dbg:
                E.wait(f"b3ts_{NTT - 1}")
                E.op("dma_start", out=P.o_dbg["t1"], in_=P.s_t1).inc(qL, 16)
                M.mark("dbg_t1", qL)
            for i in range(NTT):
                E.wait(f"b3ts_{i}")
                if i >= 2:
                    E.wait(f"c0T_{i - 2}")
                E.op("dma_start", out=P.XT[:, i % 2, :],
                     in_=P.s_t1[128 * i:128 * (i + 1), :]).inc(qL, 16)
                M.mark(f"c0x_{i}", qL)
        if nm == "scalar":
            for i in range(NTT):
                E.wait(f"c0x_{i}")
                C = P.STC
                E.op("activation", P.DUMP[:], P.XT[:, i % 2, :], AF.Square,
                     accum_out=C[:, 46:47]).inc(aS, 1)
                if i >= 1:
                    E.wait(f"c0m_{i - 1}")
                E.op("activation", C[:, 47:48], C[:, 46:47], AF.Ln,
                     scale=OneK, bias=P.SV[:, 0, 52:53])
                E.op("activation", C[:, 47:48], C[:, 47:48], AF.Exp,
                     scale=-0.5).inc(aS, 1)
                M.mark(f"c0r_{i}", aS)
        if nm == "vector":
            for i in range(NTT):
                E.wait(f"c0r_{i}")
                if i >= 2:
                    E.wait(f"c0T_{i - 2}")
                u = P.UST[:, i % 2, 0:1024]
                E.op("tensor_scalar", u, P.XT[:, i % 2, :], P.STC[:, 47:48], None,
                     ALU.mult)
                E.op("tensor_mul", u, u, P.NRM[:, 1, :]).inc(vS, 1)
                M.mark(f"c0m_{i}", vS)
                if i >= 1:
                    E.wait(f"c0T_{i - 1}")
                    b = (i - 1) % 2
                    E.op("tensor_copy",
                         P.FM4[:, :, 128 * (i - 1):128 * i],
                         P.bank(b).bitcast(BF16).rearrange("p (c t) -> p c t", c=8)).inc(vS, 1)
                    M.mark(f"c0F_{i - 1}", vS)
                    E.wait(f"c0g_{i - 1}")
                    if (i - 1) == 0 and "rt" in P.o_dbg:
                        E.wait("dbg_rt")
                    E.op("tensor_copy", P.GL[:, 8 * (i - 1):8 * i],
                         P.bank(4)[:, 128 * ((i - 1) % 4):128 * ((i - 1) % 4) + 8]).inc(vS, 1)
                    M.mark(f"c0ge_{i - 1}", vS)
            E.wait(f"c0T_{NTT - 1}")
            E.op("tensor_copy", P.FM4[:, :, 128 * (NTT - 1):128 * NTT],
                 P.bank((NTT - 1) % 2).bitcast(BF16).rearrange("p (c t) -> p c t", c=8)).inc(vS, 1)
            M.mark(f"c0F_{NTT - 1}", vS)
            E.wait(f"c0g_{NTT - 1}")
            E.op("tensor_copy", P.GL[:, 8 * (NTT - 1):8 * NTT],
                 P.bank(4)[:, 128 * ((NTT - 1) % 4):128 * ((NTT - 1) % 4) + 8]).inc(vS, 1)
            M.mark(f"c0ge_{NTT - 1}", vS)
            G = P.GL
            L3 = G[:, 0:128].rearrange("p (g e) -> p g e", e=8)
            m1 = G[:, 128:144]
            m1b = G[:, 128:144].rearrange("p g -> p g").unsqueeze(2).broadcast_to([128, 16, 8])
            m2 = G[:, 144:160]
            m2b = G[:, 144:160].unsqueeze(2).broadcast_to([128, 16, 8])
            s1 = G[:, 160:176]
            s2 = G[:, 176:192]
            s1b = G[:, 160:176].unsqueeze(2).broadcast_to([128, 16, 8])
            s2b = G[:, 176:192].unsqueeze(2).broadcast_to([128, 16, 8])
            ge1 = G[:, 192:320].rearrange("p (g e) -> p g e", e=8)
            tm = G[:, 320:448].rearrange("p (g e) -> p g e", e=8)
            comb = G[:, 448:576].rearrange("p (g e) -> p g e", e=8)
            dm = G[:, 576:592]
            E.op("tensor_reduce", m1, L3, AX.X, ALU.max)
            E.op("tensor_tensor", ge1, L3, m1b, ALU.is_ge)
            E.op("scalar_tensor_tensor", tm, ge1, -1e30, L3, ALU.mult, ALU.add)
            E.op("tensor_reduce", m2, tm, AX.X, ALU.max)
            E.op("tensor_sub", dm, m1, m2).inc(vS, 1)
            M.mark("c0gate1", vS)
            E.wait("c0sig")
            E.op("tensor_scalar", s2, s1, -1.0, 1.0, ALU.mult, ALU.add)
            E.op("tensor_tensor", comb, ge1, s1b, ALU.mult)
            E.op("tensor_tensor", tm, tm, m2b, ALU.is_ge)
            E.op("tensor_tensor", tm, tm, s2b, ALU.mult)
            E.op("tensor_add", comb, comb, tm).inc(vS, 1)
            M.mark("c0comb", vS)
        if nm == "scalar":
            E.wait("c0gate1")
            E.op("activation", P.GL[:, 160:176], P.GL[:, 576:592], AF.Sigmoid).inc(aS, 1)
            M.mark("c0sig", aS)
        if nm == "tensor":
            for i in range(NTT):
                E.wait(f"c0m_{i}")
                if i >= 2:
                    E.wait(f"c0F_{i - 2}")
                b = i % 2
                for c in range(NCC):
                    ap = P.bank(b).bitcast(BF16)[:, 128 * c:128 * (c + 1)]
                    E.op("transpose", ap, P.UST[:, i % 2, 128 * c:128 * (c + 1)],
                         P.identb[:]).inc(tS, 1 if c == NCC - 1 else 0)
                M.mark(f"c0T_{i}", tS)
                if i >= 4:
                    E.wait(f"c0ge_{i - 4}")
                E.wait(f"c0F_{i}")
                off = 128 * (i % 4)
                for c in range(NCC):
                    E.op("matmul", P.bank(4)[:, off:off + 8],
                         P.FM4[:, c, 128 * i:128 * (i + 1)],
                         P.GW[:, c, :], start=(c == 0), stop=(c == NCC - 1)).inc(
                             tS, 1 if c == NCC - 1 else 0)
                M.mark(f"c0g_{i}", tS)

        step = 0
        for e in range(NE):
            for th in range(2):
                s_w1, s_w2, s_w3 = step, step + 1, step + 2
                sl1 = P.slaV if s_w1 % 2 == 0 else P.slbV
                sl2 = P.slaV if s_w2 % 2 == 0 else P.slbV
                sl3 = P.SLA if s_w3 % 2 == 0 else P.SLB
                if nm == "sync":
                    if s_w1 >= 2:
                        E.wait(f"c1slab_{s_w1 - 2}")
                    else:
                        E.wait(f"b3o_{NTT - 1}")
                    E.wait(f"cast_w1_{e}")
                    sl1d = P.SLA if s_w1 % 2 == 0 else P.SLB
                    for c in range(NCC):
                        E.op("dma_start", out=sl1d[:, :, :].rearrange("p a b -> p (a b)")
                             .rearrange("p (c t) -> p c t", c=8)[:, c, :],
                             in_=P.s_w1[e][:, 128 * c:128 * (c + 1)], transpose=True).inc(qL, 16)
                    M.mark(f"L_s_{s_w1}", qL)
                    if s_w2 >= 2:
                        E.wait(f"c1slab_{s_w2 - 2}")
                    else:
                        E.wait(f"b3o_{NTT - 1}")
                    E.wait(f"cast_w2_{e}")
                    sl2d = P.SLA if s_w2 % 2 == 0 else P.SLB
                    for c in range(NCC):
                        E.op("dma_start", out=sl2d[:, :, :].rearrange("p a b -> p (a b)")
                             .rearrange("p (c t) -> p c t", c=8)[:, c, :],
                             in_=P.s_w2[e][:, 128 * c:128 * (c + 1)], transpose=True).inc(qL, 16)
                    M.mark(f"L_s_{s_w2}", qL)
                    E.wait(f"c1slab_{s_w3 - 2}")
                    E.wait(f"cast_w3_{e}")
                    for c in range(16):
                        E.op("dma_start", out=sl3[:, c, :],
                             in_=P.s_w3[e][:, 128 * c:128 * (c + 1)], transpose=True).inc(qL, 16)
                    M.mark(f"L_s_{s_w3}", qL)
                if nm == "scalar":
                    if e == 0 and th == 0 and "aon" in P.o_dbg:
                        E.wait("dbg_aon")
                    for hc in range(16):
                        E.wait(f"c1g1_{step}_{hc}")
                        bb = 2 * (hc % 2)
                        r1 = 13 + 2 * e + hc // 8
                        E.op("activation", P.biga16[:, hc, :],
                             P.PS[:, 512 * bb:512 * (bb + 2)], AF.Silu,
                             bias=P.sv(hc % 8, r1)).inc(aS, 1)
                        M.mark(f"c1s1_{step}_{hc}", aS)
                if th == 0:
                    if nm == "gpsimd":
                        P.qg_dma(E, P.BR[0:1, 2, :], P.i_b3_all[e].unsqueeze(0),
                                 markname=f"cast_b3_{e}")
                    if nm == "tensor":
                        E.wait("c0comb")
                        if e >= 1:
                            E.wait(f"c1cbF_{e - 1}")
                            E.wait(f"c1oe_{step - 3}_6")
                            E.wait(f"c1oe_{step - 3}_7")
                        combe = P.GL[:, 448:576].rearrange("p (g e) -> p g e", e=8)[:, :, e]
                        E.op("transpose", P.bank(4)[0:16, 0:128],
                             combe, P.identf[:]).inc(tS, 1)
                        M.mark(f"c1cbT_{e}", tS)
                        E.wait(f"c1cbe_{e}")
                        for qt in range(NTT):
                            bk = 4 + qt // 4
                            off = 128 * (qt % 4)
                            E.op("matmul", P.bank(bk)[0:1, off:off + 128],
                                 P.sel16[:, qt:qt + 1], P.E16[0:16, 0:128],
                                 start=True, stop=True).inc(tS, 1 if qt == NTT - 1 else 0)
                        M.mark(f"c1cbr_{e}", tS)
                        E.wait(f"c1cbre_{e}")
                        for n in range(4):
                            E.op("matmul", P.bank(4 + n), P.onesr[0:1, :],
                                 P.CBR[0:1, 512 * n:512 * (n + 1)], start=True,
                                 stop=True).inc(tS, 1 if n == 3 else 0)
                        M.mark(f"c1cbb_{e}", tS)
                    if nm == "vector":
                        E.wait(f"c1cbT_{e}")
                        E.op("tensor_copy", P.E16[0:16, 0:128],
                             P.bank(4)[0:16, 0:128]).inc(vS, 1)
                        M.mark(f"c1cbe_{e}", vS)
                        E.wait(f"c1cbr_{e}")
                        E.op("tensor_copy", P.CBR[0:1, :],
                             P.PS[0:1, 2048:4096]).inc(vS, 1)
                        M.mark(f"c1cbre_{e}", vS)
                        E.wait(f"c1cbb_{e}")
                        E.op("tensor_copy", P.HC[:], P.PS[:, 2048:4096]).inc(vS, 1)
                        M.mark(f"c1cbF_{e}", vS)
                if nm == "tensor":
                    E.wait(f"L_s_{s_w1}")
                    if e == 0 and th == 0:
                        E.wait(f"c0F_{NTT - 1}")
                        E.wait(f"c0g_{NTT - 1}")
                        E.wait(f"c1cbF_{e}")
                    for hc in range(16):
                        bb = 2 * (hc % 2)
                        if step == 0 and hc < 2:
                            pass
                        elif hc >= 2:
                            E.wait(f"c1s1_{step}_{hc - 2}")
                        else:
                            E.wait(f"c1h_{step - 3}_{14 + hc}")
                        for n in range(2):
                            for c in range(NCC):
                                E.op("matmul", P.bank(bb + n),
                                     sl1[:, c, 128 * hc:128 * (hc + 1)],
                                     P.FM4[:, c, 1024 * th + 512 * n:1024 * th + 512 * (n + 1)],
                                     start=(c == 0), stop=(c == NCC - 1)).inc(
                                         tS, 1 if c == NCC - 1 else 0)
                        M.mark(f"c1g1_{step}_{hc}", tS)
                    M.mark(f"c1slab_{s_w1}", tS)
                    E.wait(f"L_s_{s_w2}")
                    for hc in range(16):
                        bb = 2 * (hc % 2)
                        if hc >= 2:
                            E.wait(f"c1h_{step}_{hc - 2}")
                        else:
                            E.wait(f"c1s1_{step}_{14 + hc}")
                        for n in range(2):
                            for c in range(NCC):
                                E.op("matmul", P.bank(bb + n),
                                     sl2[:, c, 128 * hc:128 * (hc + 1)],
                                     P.FM4[:, c, 1024 * th + 512 * n:1024 * th + 512 * (n + 1)],
                                     start=(c == 0), stop=(c == NCC - 1)).inc(
                                         tS, 1 if c == NCC - 1 else 0)
                        M.mark(f"c1l2_{step}_{hc}", tS)
                    M.mark(f"c1slab_{s_w2}", tS)
                    E.wait(f"L_s_{s_w3}")
                    E.wait(f"cast_b3_{e}")
                    E.wait(f"c1cb_{step}_15")
                    for j in range(8):
                        i = 8 * th + j
                        bb = 4 + 2 * (j % 2)
                        if j >= 2:
                            E.wait(f"c1oe_{step}_{j - 2}")
                        elif th == 0 and e >= 1:
                            E.wait(f"c1oe_{step - 3}_{6 + j}")
                        elif th == 1:
                            E.wait(f"c1oe_{step - 3}_{6 + j}")
                        elif e == 0 and th == 0:
                            E.wait(f"c1cbF_{e}")
                        for n in range(2):
                            for hc in range(16):
                                E.op("matmul", P.bank(bb + n),
                                     P.biga16[:, hc, 128 * j:128 * (j + 1)],
                                     sl3[:, hc, 512 * n:512 * (n + 1)],
                                     start=(hc == 0), stop=False)
                            E.op("matmul", P.bank(bb + n),
                                 P.CBR[0:1, 128 * i:128 * (i + 1)],
                                 P.BR[0:1, 2, 512 * n:512 * (n + 1)],
                                 start=False, stop=True).inc(tS, 1)
                        M.mark(f"c1o_{step}_{j}", tS)
                    M.mark(f"c1slab_{s_w3}", tS)
                if nm == "vector":
                    for hc in range(16):
                        E.wait(f"c1l2_{step}_{hc}")
                        bb = 2 * (hc % 2)
                        r2 = 29 + 2 * e + hc // 8
                        E.op("scalar_tensor_tensor", P.biga16[:, hc, :],
                             P.PS[:, 512 * bb:512 * (bb + 2)], P.sv(hc % 8, r2),
                             P.biga16[:, hc, :], ALU.add, ALU.mult).inc(vS, 1)
                        M.mark(f"c1h_{step}_{hc}", vS)
                        E.op("tensor_mul", P.biga16[:, hc, :], P.biga16[:, hc, :],
                             P.HC[:, 1024 * th:1024 * (th + 1)]).inc(vS, 1)
                        M.mark(f"c1cb_{step}_{hc}", vS)
                    for j in range(8):
                        E.wait(f"c1o_{step}_{j}")
                        if j >= 2:
                            E.wait(f"c1acc_{step}_{j - 2}")
                        elif step >= 3:
                            E.wait(f"c1acc_{step - 3}_{6 + j}")
                        elif e == 0 and th == 0:
                            E.wait(f"c0x_{NTT - 1}")
                        bb = 4 + 2 * (j % 2)
                        E.op("tensor_copy", P.XT[:, j % 2, :],
                             P.PS[:, 512 * bb:512 * (bb + 2)]).inc(vS, 1)
                        M.mark(f"c1oe_{step}_{j}", vS)
                if nm == "gpsimd":
                    for j in range(8):
                        i = 8 * th + j
                        E.wait(f"c1oe_{step}_{j}")
                        if e == 0 and th == 0 and j == 0:
                            E.wait(f"b3ts_{NTT - 1}")
                            if "t1" in P.o_dbg:
                                E.wait("dbg_t1")
                        P.qg_dma(E, P.s_t1[128 * i:128 * (i + 1), :],
                                 P.XT[:, j % 2, :], accum=ALU.add,
                                 markname=f"c1acc_{step}_{j}")
                step += 3
        if nm == "gpsimd":
            M.mark("c1acc_all", qG)

        if nm == "gpsimd":
            for i in range(NTT):
                if i >= 2:
                    E.wait(f"c2o_{i - 2}")
                P.qg_dma(E, P.DEC[:, 1024 * (i % 2):1024 * (i % 2 + 1)],
                         P.s_t1[128 * i:128 * (i + 1), :], markname=f"c2lt_{i}")
        if nm == "sync":
            E.wait("c1acc_all")
            for i in range(NTT):
                if i >= 2:
                    E.wait(f"c2o_{i - 2}")
                    E.wait(f"c2os_{i - 2}")
                E.op("dma_start", out=P.XT[:, i % 2, :],
                     in_=P.i_x[128 * i:128 * (i + 1), :]).inc(qL, 16)
                M.mark(f"c2l_{i}", qL)
                E.wait(f"c2o_{i}")
                E.op("dma_start", out=P.o_out[128 * i:128 * (i + 1), :],
                     in_=P.DEC[:, 1024 * (i % 2):1024 * (i % 2 + 1)]).inc(qL, 16)
                M.mark(f"c2os_{i}", qL)
            sem, cnt = M.at(f"c2os_{NTT - 1}")
            E.wait_abs(qL, cnt)
        if nm == "scalar":
            E.wait(f"a4ss_{NTT - 1}")
            E.wait("c1acc_all")
            for i in range(NTT):
                if i >= 2:
                    E.wait(f"c2o_{i - 2}")
                E.op("dma_start", out=P.ST[:, i % 2, :],
                     in_=P.s_s[128 * i:128 * (i + 1), :]).inc(qA, 16)
                M.mark(f"c2ls_{i}", qA)
                E.wait(f"c2lt_{i}")
                E.op("activation", P.HS[:, 1024 * (i % 2):1024 * (i % 2 + 1)],
                     P.DEC[:, 1024 * (i % 2):1024 * (i % 2 + 1)], AF.Silu).inc(aS, 1)
                M.mark(f"c2g_{i}", aS)
        if nm == "sync" and P.o_dbg:
            sem, cnt = M.at(f"c2os_{NTT - 1}")
            E.wait_abs(qL, cnt)
            srcmap = {"xn": P.s_xn, "s": P.s_s, "t2": P.s_t1,
                      "sgv": P.s_sgv, "comb": P.GL[:, 448:576],
                      "xn2": P.FM4[:, :, :], "qr": P.INP[:],
                      "kr": P.HS[:], "mh": P.MH[:, :],
                      "sc": P.SC[:, :], "sr": P.SR[:, :]}
            ndbg = 0
            for nmv, ap in P.o_dbg.items():
                if nmv in srcmap:
                    E.op("dma_start", out=ap, in_=srcmap[nmv]).inc(qL, 16)
                    ndbg += 1
            E.wait_abs(qL, cnt + 16 * ndbg)
        if nm == "vector":
            for i in range(NTT):
                E.wait(f"c2g_{i}")
                E.wait(f"c2l_{i}")
                E.wait(f"c2ls_{i}")
                half = P.DEC[:, 1024 * (i % 2):1024 * (i % 2 + 1)]
                E.op("tensor_mul", half, P.ST[:, i % 2, :],
                     P.HS[:, 1024 * (i % 2):1024 * (i % 2 + 1)])
                E.op("tensor_add", half, half, P.XT[:, i % 2, :]).inc(vS, 1)
                M.mark(f"c2o_{i}", vS)

    def emit_engine(self, name, handle):
        E = Eng(self, name)
        E.h = handle
        self.p0(E)
        self.pA(E)
        self.pB(E)
        self.pC(E)

    def run_passes(self):
        for dry in (True, True):
            self.dry = True
            self.M.reset_pass(True)
            self.qstate = {"qL": {"mark": 0, "barrier": 0}, "qA": {"mark": 0, "barrier": 0}}
            for en in ["sync", "gpsimd", "scalar", "tensor", "vector"]:
                self._qg = 0
                self.emit_engine(en, None)
        self.dry = False
        self.M.reset_pass(False)
        self.qstate = {"qL": {"mark": 0, "barrier": 0}, "qA": {"mark": 0, "barrier": 0}}
        nc = self.nc
        with nc.Block() as block:
            def mk(en):
                def f(h):
                    self._qg = 0
                    self.emit_engine(en, h)
                return f
            block._start_engine(mk("sync"), mybir.EngineType.SP)
            block.gpsimd(mk("gpsimd"))
            block.scalar(mk("scalar"))
            block.tensor(mk("tensor"))
            block.vector(mk("vector"))

    def build(self):
        self.nc = bass.Bass("TRN2", target_bir_lowering=False)
        self.declare()
        self.alloc()
        self.run_passes()
        return self.nc


def _host_constants():
    bf = ml_dtypes.bfloat16
    c = {}
    c["identb"] = np.eye(128, dtype=bf)
    c["identf"] = np.eye(128, dtype=np.float32)
    c["sel16"] = np.eye(16, dtype=bf)
    c["onesr"] = np.ones((1, 128), dtype=bf)
    c["onesrf"] = np.ones((1, 128), dtype=np.float32)
    c["onesc"] = np.ones((128, 1), dtype=bf)
    perm = np.zeros((128, 128), dtype=np.float32)
    for d in range(128):
        base = (d // 64) * 64
        r = d % 64
        if r < 32:
            perm[d, base + r + 32] = -1.0
        else:
            perm[d, base + r - 32] = 1.0
    c["permT"] = perm.T.astype(bf)
    c["onesq"] = np.ones((128, 128), dtype=bf)
    s2 = np.zeros((128, 2), dtype=bf)
    s2[0:64, 0] = 1
    s2[64:128, 1] = 1
    c["sel2"] = s2
    c["jcol"] = (np.arange(128) % 32).astype(np.float32).reshape(128, 1)
    c["trow"] = np.arange(T, dtype=np.float32).reshape(1, T)
    sc = np.zeros((1, 8), dtype=np.float32)
    sc[0, 0] = 10000.0
    c["scals"] = sc
    return c


def _stack_vecs(inp):
    rows = [
        inp["L0_attn_ln_g"], inp["L0_attn_ln_b"], inp["L0_attn_b_conv"],
        inp["L0_attn_b_delta"], inp["L0_attn_b_B"], inp["L0_attn_b_C"],
        inp["L0_ffn_norm1_w"], inp["L0_ffn_norm2_w"],
        inp["L0_attn_b_in"][0:1024], inp["L0_attn_b_in"][1024:2048],
        inp["L0_ffn_qkv_b"][0:1024], inp["L0_ffn_qkv_b"][1024:2048],
        inp["L0_ffn_qkv_b"][2048:3072],
    ]
    for e in range(NE):
        rows += [inp["L0_ffn_b1"][e][0:1024], inp["L0_ffn_b1"][e][1024:2048]]
    for e in range(NE):
        rows += [inp["L0_ffn_b2"][e][0:1024], inp["L0_ffn_b2"][e][1024:2048]]
    for k in range(4):
        rows.append(inp["L0_attn_w_conv"][:, 0, k])
    return np.ascontiguousarray(np.stack(rows, 0).astype(np.float32, copy=False))


_NC_CACHE = {}


def kernel(**inputs):
    key = "v1"
    if key not in _NC_CACHE:
        _NC_CACHE[key] = MK().build()
    nc = _NC_CACHE[key]
    consts = _host_constants()
    vecs = _stack_vecs(inputs)
    base = {
        "w_in": inputs["L0_attn_w_in"], "w_delta": inputs["L0_attn_w_delta"],
        "w_B": inputs["L0_attn_w_B"], "w_C": inputs["L0_attn_w_C"],
        "w_out": inputs["L0_attn_w_out"], "qkv_w": inputs["L0_ffn_qkv_w"],
        "out_w": inputs["L0_ffn_out_w"], "gate_w": inputs["L0_ffn_gate_w"],
        "w1": inputs["L0_ffn_w1"], "w2": inputs["L0_ffn_w2"], "w3": inputs["L0_ffn_w3"],
        "vecs_a": vecs, "b_out_m": inputs["L0_attn_b_out"],
        "out_b_f": inputs["L0_ffn_out_b"], "b3_all": inputs["L0_ffn_b3"],
        **consts,
    }
    base = {k: np.ascontiguousarray(v) for k, v in base.items()}
    x = np.ascontiguousarray(inputs["L0_attn_x"], dtype=np.float32)
    in_maps = [dict(base, x_in=np.ascontiguousarray(x[b])) for b in range(8)]
    res = run_bass_kernel_spmd(nc, in_maps, core_ids=list(range(8)))
    out = np.stack([np.asarray(r["out"], dtype=np.float32) for r in res.results], axis=0)
    return out


# version 14 (final)
# speedup vs baseline: 1.0384x; 1.0384x over previous
import numpy as np
import ml_dtypes
import concourse.bass as bass
import concourse.mybir as mybir
from concourse.bass_utils import run_bass_kernel_spmd

F32 = mybir.dt.float32
BF16 = mybir.dt.bfloat16
I32 = mybir.dt.int32
AF = mybir.ActivationFunctionType
ALU = mybir.AluOpType
AX = mybir.AxisListType

T = 2048
D = 1024
NTT = T // 128
NCC = D // 128
NH = 16
HD = 64
NE = 8
HF = 2048
DBG = []


class Marks:

    def __init__(self, prog=None):
        self.prog = prog
        self.counts = {}
        self.marks = {}
        self.dry = True

    def reset_pass(self, dry):
        self.dry = dry
        self.counts = {}

    def bump(self, sem, n):
        self.counts[sem] = self.counts.get(sem, 0) + n

    def mark(self, name, sem):
        c = self.counts.get(sem, 0)
        if sem in ("qL", "qA"):
            self.prog.qstate[sem]["mark"] = c
        if self.dry:
            self.marks[name] = (sem, c)
        else:
            assert self.marks[name] == (sem, c), (name, self.marks[name], (sem, c))

    def at(self, name):
        return self.marks.get(name, (None, 0))


AUTO_CHAIN = {"vector": "vS", "scalar": "aS", "gpsimd": "pS"}


class Eng:

    def __init__(self, prog, name):
        self.prog = prog
        self.name = name
        self.h = None

    def op(self, fname, *args, **kw):
        auto = AUTO_CHAIN.get(self.name) if fname != "dma_start" else None
        if fname == "dma_start" and self.name in ("sync", "scalar"):
            qs = "qL" if self.name == "sync" else "qA"
            st = self.prog.qstate[qs]
            if st["mark"] > st["barrier"]:
                if not self.prog.dry:
                    self.h.wait_ge(self.prog.sems[qs], st["mark"])
                st["barrier"] = st["mark"]
        if auto is not None:
            cnt = self.prog.M.counts.get(auto, 0)
            if not self.prog.dry and cnt:
                self.h.wait_ge(self.prog.sems[auto], cnt)
        if self.prog.dry:
            inst = _DryInst(self.prog)
        else:
            inst = _RealInst(self.prog, getattr(self.h, fname)(*args, **kw))
        if auto is not None:
            self.prog.M.bump(auto, 1)
            if not self.prog.dry:
                inst.inst.then_inc(self.prog.sems[auto], 1)
        return inst

    def wait(self, markname, extra=0):
        sem, cnt = self.prog.M.at(markname)
        if not self.prog.dry:
            self.h.wait_ge(self.prog.sems[sem], cnt + extra)

    def wait_abs(self, sem, value):
        if not self.prog.dry:
            self.h.wait_ge(self.prog.sems[sem], value)


class _DryInst:
    def __init__(self, prog):
        self.prog = prog

    def inc(self, sem, n=1):
        if n and sem not in ("vS", "aS", "pS"):
            self.prog.M.bump(sem, n)
        return self


class _RealInst:
    def __init__(self, prog, inst):
        self.prog = prog
        self.inst = inst

    def inc(self, sem, n=1):
        if n and sem not in ("vS", "aS", "pS"):
            self.prog.M.bump(sem, n)
            self.inst.then_inc(self.prog.sems[sem], n)
        return self


class MK:

    def __init__(self, dbg=()):
        self.dbg = list(dbg)
        self.M = Marks(self)
        self.dry = True
        self.sems = {}
        self.nc = None
        self.qstate = {}

    def declare(self):
        nc = self.nc
        A = lambda *a, **k: nc.dram_tensor(*a, **k).ap()
        self.i_x = A("x_in", [T, D], F32, kind="ExternalInput")
        for nm, shp in [
            ("w_in", [2 * D, D]), ("w_delta", [D, D]), ("w_B", [D, D]),
            ("w_C", [D, D]), ("w_out", [D, D]), ("qkv_w", [3 * D, D]),
            ("out_w", [D, D]), ("gate_w", [NE, D]),
            ("w1", [NE, HF, D]), ("w2", [NE, HF, D]), ("w3", [NE, D, HF]),
            ("vecs_a", [49, D]), ("b_out_m", [D]), ("out_b_f", [D]),
            ("b3_all", [NE, D]),
        ]:
            setattr(self, "i_" + nm, A(nm, shp, F32, kind="ExternalInput"))
        for nm, shp in [
            ("identb", [128, 128]), ("sel16", [16, 16]), ("onesr", [1, 128]),
            ("onesc", [128, 1]), ("permT", [128, 128]), ("onesq", [128, 128]),
            ("sel2", [128, 2]),
        ]:
            setattr(self, "i_" + nm, A(nm, shp, BF16, kind="ExternalInput"))
        for nm, shp in [
            ("identf", [128, 128]), ("onesrf", [1, 128]), ("jcol", [128, 1]),
            ("trow", [1, T]), ("scals", [1, 8]),
        ]:
            setattr(self, "i_" + nm, A(nm, shp, F32, kind="ExternalInput"))
        self.s_w_in = A("s_w_in", [2 * D, D], BF16)
        self.s_w_delta = A("s_w_delta", [D, D], BF16)
        self.s_w_B = A("s_w_B", [D, D], BF16)
        self.s_w_C = A("s_w_C", [D, D], BF16)
        self.s_w_out = A("s_w_out", [D, D], BF16)
        self.s_qkv = A("s_qkv", [8, 384, D], BF16)
        self.s_out_w = A("s_out_w", [D, D], BF16)
        self.s_w1 = A("s_w1", [NE, HF, D], BF16)
        self.s_w2 = A("s_w2", [NE, HF, D], BF16)
        self.s_w3 = A("s_w3", [NE, D, HF], BF16)
        self.s_xn = A("s_xn", [T, D], F32)
        self.s_sgv = A("s_sgv", [D, T], BF16)
        self.s_s = A("s_s", [T, D], F32)
        self.s_t1 = A("s_t1", [T, D], F32)
        self.o_out = A("out", [T, D], F32, kind="ExternalOutput")
        self.o_dbg = {}
        for nm, shp, dt in self.dbg:
            self.o_dbg[nm] = A("dbg_" + nm, shp, dt, kind="ExternalOutput")

    def alloc(self):
        nc = self.nc
        self._guards = []

        def sb(name, shape, dtype):
            g = nc.sbuf_tensor(name, shape, dtype)
            h = g.__enter__()
            self._guards.append(g)
            return h

        self.identb = sb("c_identb", [128, 128], BF16)
        self.identf = sb("c_identf", [128, 128], F32)
        self.sel16 = sb("c_sel16", [16, 16], BF16)
        self.SEL2 = sb("c_sel2", [128, 2], BF16)
        self.onesr = sb("c_onesr", [1, 128], BF16)
        self.onesrf = sb("c_onesrf", [1, 128], F32)
        self.onesc = sb("c_onesc", [128, 1], BF16)
        self.permT = sb("c_permT", [128, 128], BF16)
        self.onesq = sb("c_onesq", [128, 128], BF16)
        self.jcol = sb("c_jcol", [128, 1], F32)
        self.SV = sb("SV", [128, 8, 56], F32)
        self.SC = sb("SC", [128, 16], F32)
        self.SR = sb("SR", [1, 128], F32)
        self.GW = sb("GW", [128, 8, 8], BF16)
        self.NRM = sb("NRM", [128, 2, 1024], BF16)
        self.MH = sb("MH", [128, 18], F32)
        self.E16 = sb("E16", [16, 256], BF16)
        self.STC = sb("STC", [128, 48], F32)

        self.BR = sb("BR", [1, 3, D], BF16)
        self.YS = sb("YS", [1, T], F32)
        self.RB = sb("RB", [1, T], BF16)
        self.RT = sb("RT", [128, 2, T], BF16)
        self.FM4 = sb("FM4", [128, NCC, T], BF16)
        self.SLA = sb("SLA", [128, 16, 1024], BF16)
        self.SLB = sb("SLB", [128, 16, 1024], BF16)
        self.BIGA = sb("BIGA", [128, 16384], BF16)
        self.DEC = sb("DEC", [128, T], F32)
        self.INP = sb("INP", [128, T], BF16)
        self.HS = sb("HS", [128, T], BF16)
        self.HC = sb("HC", [128, T], BF16)
        self.PT2 = sb("PT2", [128, T], BF16)
        self.UST = sb("UST", [128, 2, 1024], BF16)
        self.XT = sb("XT", [128, 2, 1024], F32)
        self.ST = sb("ST", [128, 2, 1024], F32)
        pg = nc.psum_tensor("PS", [128, 4096], F32)
        self.PS = pg.__enter__()
        self._guards.append(pg)

        for s in ["qL", "qA", "qG", "qG2", "qG3", "tS", "vS", "aS", "pS"]:
            g = nc.semaphore(s)
            self.sems[s] = g.__enter__()
            self._guards.append(g)

        slbF = self.SLB[:, :, :].rearrange("p a b -> p (a b)")
        self.QKT = slbF[:, 0:6144].rearrange("p (s i o) -> p s i o", s=2, i=8)
        self.VPs = [slbF[:, 6144:8256].rearrange("p (a b) -> p a b", a=16),
                    slbF[:, 12352:14464].rearrange("p (a b) -> p a b", a=16)]
        self.QA = slbF[:, 8256:10304]
        self.KA = slbF[:, 10304:12352]
        self.GL = self.RT[:, :, :].rearrange("p a b -> p (a b)").bitcast(F32)[:, 0:640]
        self.CBR = self.RB[0:1, :]
        self.DUMP = self.PT2[:, 0:1024]
        self.XTf = self.XT[:, :, :].rearrange("p a b -> p (a b)")
        self.STf = self.ST[:, :, :].rearrange("p a b -> p (a b)")
        self.DECb = self.DEC[:, :].bitcast(BF16)
        self.biga8 = self.BIGA[:, :].rearrange("p (c t) -> p c t", c=8)
        self.biga16 = self.BIGA[:, :].rearrange("p (c t) -> p c t", c=16)
        self.slaV = self.SLA[:, :, :].rearrange("p a b -> p (a b)").rearrange(
            "p (c t) -> p c t", c=8)
        self.slbV = self.SLB[:, :, :].rearrange("p a b -> p (a b)").rearrange(
            "p (c t) -> p c t", c=8)

    def bank(self, i, n=1):
        return self.PS[:, 512 * i: 512 * (i + n)]

    def qg_dma(self, E, out, in_, accum=None, markname=None, q="qG"):
        E.wait_abs(q, 16 * self._qg[q])
        kw = {}
        if accum is not None:
            kw["accum_op"] = accum
        E.op("dma_start", out=out, in_=in_, **kw).inc(q, 16)
        self._qg[q] += 1
        if markname:
            self.M.mark(markname, q)

    def sv(self, chunk, col):
        return self.SV[:, chunk, col: col + 1]


    def p0(self, E):
        P, nm = self, E.name
        qL, qA, qG, tS, vS, aS, pS = "qL", "qA", "qG", "tS", "vS", "aS", "pS"
        M = self.M
        ROWS = [6, 7, 0, 1]
        if nm == "sync":
            for dst, srcap in [
                (P.identb[:], P.i_identb), (P.identf[:], P.i_identf),
                (P.sel16[:], P.i_sel16), (P.onesr[:], P.i_onesr),
                (P.onesrf[:], P.i_onesrf), (P.onesc[:], P.i_onesc),
                (P.permT[:], P.i_permT), (P.jcol[:], P.i_jcol),
                (P.onesq[:], P.i_onesq), (P.SEL2[:], P.i_sel2),
                (P.STf[0:1, :], P.i_trow), (P.SR[0:1, 120:128], P.i_scals),
                (P.XT[0:49, 0, :], P.i_vecs_a),
                (P.XT[0:8, 1, :], P.i_gate_w),
            ]:
                E.op("dma_start", out=dst, in_=srcap).inc(qL, 16)
            M.mark("L_consts", qL)
        if nm == "gpsimd":
            casts = [
                ("w_in", P.s_w_in, P.i_w_in),
                ("w_delta", P.s_w_delta, P.i_w_delta),
                ("w_B", P.s_w_B, P.i_w_B),
                ("w_C", P.s_w_C, P.i_w_C),
                ("w_out", P.s_w_out, P.i_w_out),
                ("qkv_0", P.s_qkv[:, 0:128, :],
                 P.i_qkv_w[0:1024].rearrange("(hp r) c -> hp r c", hp=8)),
                ("qkv_1", P.s_qkv[:, 128:256, :],
                 P.i_qkv_w[1024:2048].rearrange("(hp r) c -> hp r c", hp=8)),
                ("qkv_2", P.s_qkv[:, 256:384, :],
                 P.i_qkv_w[2048:3072].rearrange("(hp r) c -> hp r c", hp=8)),
                ("out_w", P.s_out_w, P.i_out_w),
                ("b_out_m", P.BR[0:1, 0, :], P.i_b_out_m.unsqueeze(0)),
                ("out_b_f", P.BR[0:1, 1, :], P.i_out_b_f.unsqueeze(0)),
            ]
            for ci, (cn, dst, srcap) in enumerate(casts):
                P.qg_dma(E, dst, srcap, markname=f"cast_{cn}",
                         q="qG2" if ci % 2 == 0 else "qG3")
        if nm == "scalar":
            E.wait("L_consts")
            E.wait("v_p0_memset")
            E.op("activation", P.SR[0:1, 0:1], P.SR[0:1, 120:121], AF.Ln).inc(aS, 1)
            M.mark("a_ln4", aS)
            E.op("activation", P.SR[0:1, 2:3], P.SR[0:1, 120:121], AF.Arctan,
                 scale=0.0, bias=P.SV[0:1, 0, 53:54]).inc(aS, 1)
            M.mark("a_pi4", aS)
            E.wait("v_sccols")
            E.op("activation", P.SC[:, 0:1], P.jcol[:], AF.Exp, scale=P.SC[:, 1:2]).inc(aS, 1)
            M.mark("a_invf", aS)
            E.wait("v_red_sin")
            E.op("activation", P.RT[:, 1, :], P.STf, AF.Sin).inc(aS, 1)
            M.mark("a_sin", aS)
            E.wait("v_red_cos")
            E.op("activation", P.RT[:, 0, :], P.DEC[:], AF.Sin).inc(aS, 1)
            M.mark("a_tables", aS)
        if nm == "tensor":
            E.wait("L_consts")
            for c in range(NCC):
                if c >= 2:
                    E.wait(f"v_svT_{c - 2}")
                E.op("transpose", P.bank(c % 2)[:, 0:49],
                     P.XT[0:49, 0, 128 * c:128 * (c + 1)], P.identf[0:49, 0:49]).inc(tS, 1)
                M.mark(f"t_svT_{c}", tS)
            for c in range(NCC):
                if c >= 2:
                    E.wait(f"v_gwT_{c - 2}")
                E.op("transpose", P.bank(2 + c % 2)[:, 0:8],
                     P.XT[0:8, 1, 128 * c:128 * (c + 1)], P.identf[0:8, 0:8]).inc(tS, 1)
                M.mark(f"t_gwT_{c}", tS)
            for j, r in enumerate(ROWS):
                if j >= 1:
                    E.wait(f"v_rowe_{j - 1}")
                h = 1024 * (j % 2)
                E.op("matmul", P.bank(4)[0:1, 0:512], P.identf[0:49, r:r + 1],
                     P.XT[0:49, 0, 0:512], start=True, stop=True)
                E.op("matmul", P.bank(5)[0:1, 0:512], P.identf[0:49, r:r + 1],
                     P.XT[0:49, 0, 512:1024], start=True, stop=True).inc(tS, 1)
                M.mark(f"t_row_{j}", tS)
                if j < 3:
                    E.wait(f"v_rowe_{j}")
                    if j >= 1:
                        E.wait(f"v_bc_{j - 1}")
                    E.op("matmul", P.bank(6)[:, :], P.onesrf[0:1, :],
                         P.YS[0:1, h:h + 512], start=True, stop=True)
                    E.op("matmul", P.bank(7)[:, :], P.onesrf[0:1, :],
                         P.YS[0:1, h + 512:h + 1024], start=True, stop=True).inc(tS, 1)
                    M.mark(f"t_bc_{j}", tS)
            E.wait("v_sc_ready")
            for jj, cell in enumerate([1, 4, 5, 6, 7]):
                E.op("matmul", P.bank(4)[:, jj:jj + 1], P.onesrf[0:1, :],
                     P.SR[0:1, cell:cell + 1], start=True, stop=True).inc(tS, 1)
            M.mark("t_sccols", tS)
            E.wait("a_invf")
            E.op("transpose", P.bank(5)[0:1, 0:128], P.SC[:, 0:1], P.identf[:]).inc(tS, 1)
            M.mark("t_invr", tS)
            E.wait("v_invr")
            for n in range(4):
                E.op("matmul", P.bank(0 + n)[:, :], P.SR[0:1, 0:128],
                     P.STf[0:1, 512 * n:512 * (n + 1)], start=True, stop=True).inc(tS, 1)
            M.mark("t_freqs", tS)
            E.wait("v_red_cos")
            E.wait(f"v_rowe_3")
            E.op("matmul", P.bank(2)[:, :], P.onesrf[0:1, :],
                 P.YS[0:1, 1024:1536], start=True, stop=True)
            E.op("matmul", P.bank(3)[:, :], P.onesrf[0:1, :],
                 P.YS[0:1, 1536:2048], start=True, stop=True).inc(tS, 1)
            M.mark("t_bc_3", tS)
        if nm == "vector":
            E.op("memset", P.SV[:, 0, 51:52], 1e-5)
            E.op("memset", P.SV[:, 0, 52:53], 1e-6)
            E.op("memset", P.SV[:, 0, 53:54], 1.0).inc(vS, 1)
            M.mark("v_p0_memset", vS)
            for c in range(NCC):
                E.wait(f"t_svT_{c}")
                E.op("tensor_copy", P.SV[:, c, 0:49], P.bank(c % 2)[:, 0:49]).inc(vS, 1)
                M.mark(f"v_svT_{c}", vS)
            for c in range(NCC):
                E.wait(f"t_gwT_{c}")
                E.op("tensor_copy", P.GW[:, c, :], P.bank(2 + c % 2)[:, 0:8]).inc(vS, 1)
                M.mark(f"v_gwT_{c}", vS)
            E.op("tensor_scalar_mul", P.SV[:, :, 49:50], P.SV[:, :, 3:4], -1.0)
            E.op("tensor_scalar_mul", P.SV[:, :, 50:51], P.SV[:, :, 11:12], 0.125).inc(vS, 1)
            M.mark("v_svd", vS)
            for j in range(4):
                h = 1024 * (j % 2)
                E.wait(f"t_row_{j}")
                E.op("tensor_copy", P.YS[0:1, h:h + 512], P.PS[0:1, 4 * 512:4 * 512 + 512])
                E.op("tensor_copy", P.YS[0:1, h + 512:h + 1024],
                     P.PS[0:1, 5 * 512:5 * 512 + 512]).inc(vS, 1)
                M.mark(f"v_rowe_{j}", vS)
                if j < 2:
                    E.wait(f"t_bc_{j}")
                    E.op("tensor_copy", P.NRM[:, j, 0:512], P.bank(6))
                    E.op("tensor_copy", P.NRM[:, j, 512:1024], P.bank(7)).inc(vS, 1)
                    M.mark(f"v_bc_{j}", vS)
            E.wait("a_ln4")
            E.op("tensor_scalar_mul", P.SR[0:1, 1:2], P.SR[0:1, 0:1], -0.03125).inc(vS, 1)
            E.wait("a_pi4")
            E.op("tensor_scalar_mul", P.SR[0:1, 3:4], P.SR[0:1, 2:3], 8.0)
            E.op("reciprocal", P.SR[0:1, 4:5], P.SR[0:1, 3:4])
            E.op("tensor_scalar_mul", P.SR[0:1, 5:6], P.SR[0:1, 3:4], -1.0)
            E.op("tensor_scalar_mul", P.SR[0:1, 6:7], P.SR[0:1, 2:3], 2.0)
            E.op("tensor_scalar_mul", P.SR[0:1, 7:8], P.SR[0:1, 2:3], 4.0).inc(vS, 1)
            M.mark("v_sc_ready", vS)
            E.wait("t_sccols")
            E.op("tensor_copy", P.SC[:, 1:2], P.bank(4)[:, 0:1])
            E.op("tensor_copy", P.SC[:, 2:3], P.bank(4)[:, 1:2])
            E.op("tensor_copy", P.SC[:, 3:4], P.bank(4)[:, 2:3])
            E.op("tensor_copy", P.SC[:, 4:5], P.bank(4)[:, 3:4])
            E.op("tensor_copy", P.SC[:, 5:6], P.bank(4)[:, 4:5]).inc(vS, 1)
            M.mark("v_sccols", vS)
            E.wait("t_invr")
            E.op("tensor_copy", P.SR[0:1, 0:128], P.bank(5)[0:1, 0:128]).inc(vS, 1)
            M.mark("v_invr", vS)
            E.wait("t_freqs")
            FR = P.PS[:, 0:2048]
            KI = P.XTf.bitcast(I32)
            E.op("tensor_scalar", P.DEC[:], FR, P.SC[:, 2:3], None, ALU.mult)
            E.op("tensor_copy", KI, P.DEC[:])
            E.op("tensor_copy", P.DEC[:], KI)
            E.op("scalar_tensor_tensor", P.STf, P.DEC[:], P.SC[:, 3:4], FR,
                 ALU.mult, ALU.add)
            E.op("tensor_scalar", P.XTf, P.STf, P.SC[:, 5:6], None, ALU.is_gt)
            E.op("scalar_tensor_tensor", P.STf, P.XTf, P.SC[:, 3:4], P.STf,
                 ALU.mult, ALU.add).inc(vS, 1)
            M.mark("v_red_sin", vS)
            E.wait("a_sin")
            E.op("tensor_scalar", P.STf, FR, P.SC[:, 4:5], None, ALU.add)
            E.op("tensor_scalar", P.DEC[:], P.STf, P.SC[:, 2:3], None, ALU.mult)
            E.op("tensor_copy", KI, P.DEC[:])
            E.op("tensor_copy", P.DEC[:], KI)
            E.op("scalar_tensor_tensor", P.DEC[:], P.DEC[:], P.SC[:, 3:4], P.STf,
                 ALU.mult, ALU.add)
            E.op("tensor_scalar", P.XTf, P.DEC[:], P.SC[:, 5:6], None, ALU.is_gt)
            E.op("scalar_tensor_tensor", P.DEC[:], P.XTf, P.SC[:, 3:4], P.DEC[:],
                 ALU.mult, ALU.add).inc(vS, 1)
            M.mark("v_red_cos", vS)
            E.wait("a_tables")
            E.wait("t_bc_2")
            E.op("tensor_copy", P.DEC[:, 0:512], P.bank(6))
            E.op("tensor_copy", P.DEC[:, 512:1024], P.bank(7)).inc(vS, 1)
            E.wait("t_bc_3")
            E.op("tensor_copy", P.DEC[:, 1024:1536], P.bank(2))
            E.op("tensor_copy", P.DEC[:, 1536:2048], P.bank(3)).inc(vS, 1)
            M.mark("v_lnglnb", vS)

    def pA(self, E):
        P, nm = self, E.name
        qL, qA, qG, tS, vS, aS, pS = "qL", "qA", "qG", "tS", "vS", "aS", "pS"
        M = self.M
        OneK = 1.0 / 1024.0

        if nm == "sync":
            for i in range(NTT):
                if i == 0:
                    E.wait("v_red_cos")
                    E.wait("t_bc_3")
                if i >= 2:
                    E.wait(f"a1T_{i - 2}")
                    E.wait(f"a1xns_{i - 2}")
                E.op("dma_start", out=P.XT[:, i % 2, :],
                     in_=P.i_x[128 * i:128 * (i + 1), :]).inc(qL, 16)
                M.mark(f"a1x_{i}", qL)
                E.wait(f"a1xn_{i}")
                E.op("dma_start", out=P.s_xn[128 * i:128 * (i + 1), :],
                     in_=P.XT[:, i % 2, :]).inc(qL, 16)
                M.mark(f"a1xns_{i}", qL)
        if nm == "vector":
            for i in range(NTT):
                E.wait(f"a1x_{i}")
                xt = P.XT[:, i % 2, :]
                C = P.STC
                E.op("reduce_sum", C[:, 32:33], xt, axis=AX.X)
                E.wait(f"a1sq_{i}")
                E.op("tensor_scalar_mul", C[:, 34:35], C[:, 32:33], OneK)
                E.op("tensor_mul", C[:, 35:36], C[:, 34:35], C[:, 34:35])
                E.op("scalar_tensor_tensor", C[:, 36:37], C[:, 33:34], OneK,
                     C[:, 35:36], ALU.mult, ALU.subtract).inc(vS, 1)
                M.mark(f"a1var_{i}", vS)
                E.wait(f"a1rstd_{i}")
                E.op("tensor_scalar", xt, xt, C[:, 34:35], C[:, 37:38],
                     ALU.subtract, ALU.mult).inc(vS, 1)
                M.mark(f"a1xn_{i}", vS)
                E.op("reduce_sum", C[:, 38:39], xt, axis=AX.X)
                E.wait(f"a1sq2_{i}")
                if i >= 1:
                    E.wait(f"a1m_{i - 1}")
                E.op("tensor_scalar_mul", C[:, 39:40], C[:, 38:39], OneK)
                E.op("tensor_mul", C[:, 40:41], C[:, 39:40], C[:, 39:40])
                E.op("scalar_tensor_tensor", C[:, 41:42], C[:, 33:34], OneK,
                     C[:, 40:41], ALU.mult, ALU.subtract).inc(vS, 1)
                M.mark(f"a1var2_{i}", vS)
                E.wait(f"a1rstd2_{i}")
                if i >= 2:
                    E.wait(f"a1T_{i - 2}")
                u = P.UST[:, i % 2, 0:1024]
                E.op("tensor_scalar", u, xt, C[:, 39:40], C[:, 42:43],
                     ALU.subtract, ALU.mult)
                E.op("tensor_mul", u, u, P.DEC[:, 0:1024])
                E.op("tensor_add", u, u, P.DEC[:, 1024:2048]).inc(vS, 1)
                M.mark(f"a1m_{i}", vS)
                if i >= 1:
                    E.wait(f"a1T_{i - 1}")
                    b = 4 + (i - 1) % 2
                    E.op("tensor_copy",
                         P.FM4[:, :, 128 * (i - 1):128 * i],
                         P.bank(b).bitcast(BF16).rearrange("p (c t) -> p c t", c=8)).inc(vS, 1)
                    M.mark(f"a1F_{i - 1}", vS)
            E.wait(f"a1T_{NTT - 1}")
            E.op("tensor_copy", P.FM4[:, :, 128 * (NTT - 1):128 * NTT],
                 P.bank(4 + (NTT - 1) % 2).bitcast(BF16).rearrange("p (c t) -> p c t", c=8)).inc(vS, 1)
            M.mark(f"a1F_{NTT - 1}", vS)
        if nm == "scalar":
            for i in range(NTT):
                E.wait(f"a1x_{i}")
                if i >= 1:
                    E.wait(f"a1var2_{i - 1}")
                xt = P.XT[:, i % 2, :]
                C = P.STC
                E.op("activation", P.DUMP[:], xt, AF.Square,
                     accum_out=C[:, 33:34]).inc(aS, 1)
                M.mark(f"a1sq_{i}", aS)
                E.wait(f"a1var_{i}")
                E.op("activation", C[:, 37:38], C[:, 36:37], AF.Ln,
                     bias=P.SV[:, 0, 51:52])
                E.op("activation", C[:, 37:38], C[:, 37:38], AF.Exp,
                     scale=-0.5).inc(aS, 1)
                M.mark(f"a1rstd_{i}", aS)
                E.wait(f"a1xn_{i}")
                E.op("activation", P.DUMP[:], xt, AF.Square,
                     accum_out=C[:, 33:34]).inc(aS, 1)
                M.mark(f"a1sq2_{i}", aS)
                E.wait(f"a1var2_{i}")
                if i >= 1:
                    E.wait(f"a1m_{i - 1}")
                E.op("activation", C[:, 42:43], C[:, 41:42], AF.Ln,
                     bias=P.SV[:, 0, 51:52])
                E.op("activation", C[:, 42:43], C[:, 42:43], AF.Exp,
                     scale=-0.5).inc(aS, 1)
                M.mark(f"a1rstd2_{i}", aS)
                if i < 4:
                    if i == 0:
                        E.wait("cast_w_in")
                    for c in (2 * i, 2 * i + 1):
                        E.op("dma_start", out=P.slaV[:, c, :],
                             in_=P.s_w_in[:, 128 * c:128 * (c + 1)],
                             transpose=True).inc(qA, 16)
                    if i == 3:
                        M.mark("L_winT", qA)
                elif 6 <= i < 10:
                    if i == 6:
                        E.wait("cast_w_delta")
                    for c in (2 * (i - 6), 2 * (i - 6) + 1):
                        E.op("dma_start", out=P.SLB[:, c, :],
                             in_=P.s_w_delta[:, 128 * c:128 * (c + 1)],
                             transpose=True).inc(qA, 16)
                elif 10 <= i < 14:
                    if i == 10:
                        E.wait("cast_w_B")
                    for c in (2 * (i - 10), 2 * (i - 10) + 1):
                        E.op("dma_start", out=P.SLB[:, 8 + c, :],
                             in_=P.s_w_B[:, 128 * c:128 * (c + 1)],
                             transpose=True).inc(qA, 16)
                    if i == 13:
                        M.mark("L_wdB", qA)
        if nm == "tensor":
            for i in range(NTT):
                E.wait(f"a1m_{i}")
                if i >= 2:
                    E.wait(f"a1F_{i - 2}")
                b = 4 + i % 2
                for c in range(NCC):
                    ap = P.bank(b).bitcast(BF16)[:, 128 * c:128 * (c + 1)]
                    E.op("transpose", ap, P.UST[:, i % 2, 128 * c:128 * (c + 1)],
                         P.identb[:]).inc(tS, 1 if c == NCC - 1 else 0)
                M.mark(f"a1T_{i}", tS)

        ust2 = lambda o: (P.HC if o % 2 == 0 else P.PT2)[:, :]
        if nm == "tensor":
            E.wait("L_winT")
            for o in range(16):
                for n in range(4):
                    g = o * 4 + n
                    b = g % 4
                    if o == 0:
                        E.wait(f"a1F_{4 * n + 3}")
                    if g >= 4:
                        E.wait(f"a2e_{g - 4}")
                    for ic in range(NCC):
                        E.op("matmul", P.bank(b), P.slaV[:, ic, 128 * o:128 * (o + 1)],
                             P.FM4[:, ic, 512 * n:512 * (n + 1)],
                             start=(ic == 0), stop=(ic == NCC - 1)).inc(
                                 tS, 1 if ic == NCC - 1 else 0)
                    M.mark(f"a2g_{g}", tS)
        if nm == "scalar":
            for o in range(16):
                if o >= 2:
                    if o - 2 < 8:
                        E.wait(f"a2c_{o - 2}")
                    else:
                        E.wait(f"a2vs_{o - 2}")
                for n in range(4):
                    g = o * 4 + n
                    E.wait(f"a2g_{g}")
                    dst = ust2(o)[:, 512 * n:512 * (n + 1)]
                    if o < 8:
                        E.op("activation", dst, P.bank(g % 4), AF.Identity,
                             bias=P.sv(o, 8)).inc(aS, 1)
                    else:
                        E.op("activation", dst, P.bank(g % 4), AF.Sigmoid,
                             bias=P.sv(o - 8, 9)).inc(aS, 1)
                    M.mark(f"a2e_{g}", aS)
                M.mark(f"a2eo_{o}", aS)
        if nm == "vector":
            for o in range(8):
                E.wait(f"a2eo_{o}")
                uc = P.biga8[:, o, :]
                u = ust2(o)
                E.op("tensor_scalar", uc, u, P.sv(o, 48), P.sv(o, 2),
                     ALU.mult, ALU.add)
                E.op("scalar_tensor_tensor", uc[:, 1:T], u[:, 0:T - 1], P.sv(o, 47),
                     uc[:, 1:T], ALU.mult, ALU.add)
                E.op("scalar_tensor_tensor", uc[:, 2:T], u[:, 0:T - 2], P.sv(o, 46),
                     uc[:, 2:T], ALU.mult, ALU.add)
                E.op("scalar_tensor_tensor", uc[:, 3:T], u[:, 0:T - 3], P.sv(o, 45),
                     uc[:, 3:T], ALU.mult, ALU.add).inc(vS, 1)
                M.mark(f"a2c_{o}", vS)
        if nm == "sync":
            for o in range(8, 16):
                E.wait(f"a2eo_{o}")
                E.op("dma_start", out=P.s_sgv[128 * (o - 8):128 * (o - 7), :],
                     in_=ust2(o)).inc(qL, 16)
                M.mark(f"a2vs_{o}", qL)
            E.wait("cast_w_out")
            E.wait("a2g_63")
            for c in range(NCC):
                E.op("dma_start", out=P.SLA[:, c, :],
                     in_=P.s_w_C[:, 128 * c:128 * (c + 1)], transpose=True).inc(qL, 16)
            for c in range(NCC):
                E.op("dma_start", out=P.SLA[:, 8 + c, :],
                     in_=P.s_w_out[:, 128 * c:128 * (c + 1)], transpose=True).inc(qL, 16)
            M.mark("L_wCo", qL)

        if nm == "tensor":
            E.wait("L_wdB")
            for s in range(NCC):
                if s == 0:
                    E.wait("a2eo_15")
                if s >= 1:
                    E.wait(f"a3hc_{s - 1}")
                for n in range(4):
                    for ic in range(NCC):
                        E.op("matmul", P.bank(n), P.SLB[:, ic, 128 * s:128 * (s + 1)],
                             P.biga8[:, ic, 512 * n:512 * (n + 1)],
                             start=(ic == 0), stop=(ic == NCC - 1)).inc(
                                 tS, 1 if ic == NCC - 1 else 0)
                M.mark(f"a3z_{s}", tS)
                if s >= 1:
                    E.wait(f"a3ys_{s - 1}")
                for n in range(4):
                    for ic in range(NCC):
                        E.op("matmul", P.bank(4 + n), P.SLB[:, 8 + ic, 128 * s:128 * (s + 1)],
                             P.biga8[:, ic, 512 * n:512 * (n + 1)],
                             start=(ic == 0), stop=(ic == NCC - 1)).inc(
                                 tS, 1 if ic == NCC - 1 else 0)
                M.mark(f"a3b_{s}", tS)
                E.wait(f"a3d_{s}")
                if s == 0:
                    E.wait("L_wCo")
                for n in range(4):
                    for ic in range(NCC):
                        E.op("matmul", P.bank(n), P.SLA[:, ic, 128 * s:128 * (s + 1)],
                             P.biga8[:, ic, 512 * n:512 * (n + 1)],
                             start=(ic == 0), stop=(ic == NCC - 1)).inc(
                                 tS, 1 if ic == NCC - 1 else 0)
                M.mark(f"a3c_{s}", tS)
                E.wait(f"a3hc_{s}")
                E.wait(f"a3i_{s}")
                for n in range(4):
                    E.op("matmul", P.bank(4 + n)[0:1, :], P.onesc[:],
                         P.HC[:, 512 * n:512 * (n + 1)], start=True, stop=True).inc(
                             tS, 1 if n == 3 else 0)
                M.mark(f"a3y_{s}", tS)
        if nm == "scalar":
            for s in range(NCC):
                E.wait(f"a3z_{s}")
                E.wait(f"a3dc_{s - 1}") if s >= 1 else None
                E.op("activation", P.DEC[:], P.PS[:, 0:2048], AF.Sigmoid,
                     scale=-1.0, bias=P.sv(s, 49)).inc(aS, 1)
                M.mark(f"a3d_{s}", aS)
        if nm == "vector":
            for s in range(NCC):
                E.wait(f"a3b_{s}")
                E.op("scalar_tensor_tensor", P.INP[:], P.PS[:, 2048:4096], P.sv(s, 4),
                     P.biga8[:, s, :], ALU.add, ALU.mult).inc(vS, 1)
                M.mark(f"a3i_{s}", vS)
                E.wait(f"a3d_{s}")
                E.op("tensor_tensor_scan", P.HS[:], P.DEC[:], P.INP[:], 0.0,
                     ALU.mult, ALU.add).inc(vS, 1)
                M.mark(f"a3dc_{s}", vS)
                E.wait(f"a3c_{s}")
                if s == 0:
                    E.wait("a2vs_14")
                E.op("scalar_tensor_tensor", P.HC[:], P.PS[:, 0:2048], P.sv(s, 5),
                     P.HS[:], ALU.add, ALU.mult).inc(vS, 1)
                M.mark(f"a3hc_{s}", vS)
                E.wait(f"a3y_{s}")
                if s == 0:
                    E.op("tensor_copy", P.YS[0:1, :], P.PS[0:1, 2048:4096]).inc(vS, 1)
                else:
                    E.op("tensor_add", P.YS[0:1, :], P.YS[0:1, :],
                         P.PS[0:1, 2048:4096]).inc(vS, 1)
                M.mark(f"a3ys_{s}", vS)

        if nm == "vector":
            E.op("tensor_copy", P.RB[0:1, :], P.YS[0:1, :]).inc(vS, 1)
            M.mark("a4rb", vS)
        if nm == "tensor":
            E.wait("a4rb")
            E.wait(f"a3hc_{NCC - 1}")
            for n in range(4):
                E.op("matmul", P.bank(n), P.onesr[0:1, :],
                     P.RB[0:1, 512 * n:512 * (n + 1)], start=True, stop=True).inc(
                         tS, 1 if n == 3 else 0)
            M.mark("a4bc", tS)
        if nm == "sync":
            for s in range(NCC):
                if s == 0:
                    E.wait("a2vs_15")
                    E.wait(f"a3y_{NCC - 1}")
                if s == 1:
                    E.wait("a2c_7")
                if s >= 2:
                    E.wait(f"a4y_{s - 2}")
                buf = P.HC if s % 2 == 0 else P.PT2
                E.op("dma_start", out=buf[:],
                     in_=P.s_sgv[128 * s:128 * (s + 1), :]).inc(qL, 16)
                M.mark(f"a4g_{s}", qL)
        if nm == "vector":
            for s in range(NCC):
                E.wait(f"a4g_{s}")
                if s == 0:
                    E.wait("a4bc")
                buf = P.HC if s % 2 == 0 else P.PT2
                E.op("tensor_mul", P.biga8[:, s, :], P.PS[:, 0:2048], buf[:]).inc(vS, 1)
                M.mark(f"a4y_{s}", vS)
        if nm == "sync":
            for i in range(NTT):
                if i == 0:
                    E.wait(f"a1xns_{NTT - 1}")
                if i >= 2:
                    E.wait(f"a4s_{i - 2}")
                    E.wait(f"a4ss_{i - 2}")
                E.op("dma_start", out=P.XT[:, i % 2, :],
                     in_=P.s_xn[128 * i:128 * (i + 1), :]).inc(qL, 16)
                M.mark(f"a4x_{i}", qL)
                E.wait(f"a4s_{i}")
                E.op("dma_start", out=P.s_s[128 * i:128 * (i + 1), :],
                     in_=P.ST[:, i % 2, :]).inc(qL, 16)
                M.mark(f"a4ss_{i}", qL)
        if nm == "tensor":
            E.wait(f"a4y_{NCC - 1}")
            E.wait("cast_b_out_m")
            for i in range(NTT):
                bb = 4 + 2 * (i % 2)
                if i >= 2:
                    E.wait(f"a4s_{i - 2}")
                for n in range(2):
                    for s in range(NCC):
                        E.op("matmul", P.bank(bb + n),
                             P.biga8[:, s, 128 * i:128 * (i + 1)],
                             P.SLA[:, 8 + s, 512 * n:512 * (n + 1)],
                             start=(s == 0), stop=False)
                    E.op("matmul", P.bank(bb + n), P.onesr[0:1, 0:128],
                         P.BR[0:1, 0, 512 * n:512 * (n + 1)],
                         start=False, stop=True).inc(tS, 1)
                M.mark(f"a4o_{i}", tS)
        if nm == "vector":
            for i in range(NTT):
                E.wait(f"a4o_{i}")
                E.wait(f"a4x_{i}")
                bb = 4 + 2 * (i % 2)
                E.op("tensor_add", P.ST[:, i % 2, :], P.XT[:, i % 2, :],
                     P.PS[:, 512 * bb:512 * (bb + 2)]).inc(vS, 1)
                M.mark(f"a4s_{i}", vS)

    def pB(self, E):
        P, nm = self, E.name
        qL, qA, qG, tS, vS, aS, pS = "qL", "qA", "qG", "tS", "vS", "aS", "pS"
        M = self.M
        OneK = 1.0 / 1024.0
        QR = self.INP.bitcast(BF16)[:, 0:2048]
        KR = self.HS.bitcast(BF16)[:, 0:2048]

        if nm == "sync":
            for i in range(NTT):
                if i == 0:
                    E.wait(f"a4s_{NTT - 2}")
                    E.wait(f"a4s_{NTT - 1}")
                if i >= 2:
                    E.wait(f"b0T_{i - 2}")
                E.op("dma_start", out=P.XT[:, i % 2, :],
                     in_=P.s_xn[128 * i:128 * (i + 1), :]).inc(qL, 16)
                M.mark(f"b0x_{i}", qL)
            E.wait("cast_out_w")
            E.wait(f"a4o_{NTT - 1}")
            for c in range(NCC):
                E.op("dma_start", out=P.SLA[:, c, :],
                     in_=P.s_out_w[:, 128 * c:128 * (c + 1)], transpose=True).inc(qL, 16)
            M.mark("L_owT", qL)
        if nm == "scalar":
            for i in range(NTT):
                E.wait(f"b0x_{i}")
                C = P.STC
                E.op("activation", P.DUMP[:], P.XT[:, i % 2, :], AF.Square,
                     accum_out=C[:, 44:45]).inc(aS, 1)
                if i >= 1:
                    E.wait(f"b0m_{i - 1}")
                E.op("activation", C[:, 45:46], C[:, 44:45], AF.Ln,
                     scale=OneK, bias=P.SV[:, 0, 52:53])
                E.op("activation", C[:, 45:46], C[:, 45:46], AF.Exp,
                     scale=-0.5).inc(aS, 1)
                M.mark(f"b0r_{i}", aS)
        if nm == "vector":
            for i in range(NTT):
                E.wait(f"b0r_{i}")
                if i >= 2:
                    E.wait(f"b0T_{i - 2}")
                u = P.UST[:, i % 2, 0:1024]
                E.op("tensor_scalar", u, P.XT[:, i % 2, :], P.STC[:, 45:46], None,
                     ALU.mult)
                E.op("tensor_mul", u, u, P.NRM[:, 0, :]).inc(vS, 1)
                M.mark(f"b0m_{i}", vS)
                if i >= 1:
                    E.wait(f"b0T_{i - 1}")
                    b = (i - 1) % 2
                    E.op("tensor_copy",
                         P.FM4[:, :, 128 * (i - 1):128 * i],
                         P.bank(b).bitcast(BF16).rearrange("p (c t) -> p c t", c=8)).inc(vS, 1)
                    M.mark(f"b0F_{i - 1}", vS)
            E.wait(f"b0T_{NTT - 1}")
            E.op("tensor_copy", P.FM4[:, :, 128 * (NTT - 1):128 * NTT],
                 P.bank((NTT - 1) % 2).bitcast(BF16).rearrange("p (c t) -> p c t", c=8)).inc(vS, 1)
            M.mark(f"b0F_{NTT - 1}", vS)
        if nm == "tensor":
            for i in range(NTT):
                E.wait(f"b0m_{i}")
                if i >= 2:
                    E.wait(f"b0F_{i - 2}")
                b = i % 2
                for c in range(NCC):
                    ap = P.bank(b).bitcast(BF16)[:, 128 * c:128 * (c + 1)]
                    E.op("transpose", ap, P.UST[:, i % 2, 128 * c:128 * (c + 1)],
                         P.identb[:]).inc(tS, 1 if c == NCC - 1 else 0)
                M.mark(f"b0T_{i}", tS)

        def _qkv_prefetch(hp):
            for ic in range(NCC):
                E.op("dma_start", out=P.QKT[:, hp % 2, ic, :],
                     in_=P.s_qkv[hp, :, 128 * ic:128 * (ic + 1)],
                     transpose=True).inc(qL, 16)
            M.mark(f"L_qkv_{hp}", qL)
        if nm == "vector":
            E.wait("a3b_7")
            E.op("memset", P.VPs[0][:, :, 64:65], 1.0)
            E.op("memset", P.VPs[0][:, :, 129:130], 1.0)
            E.op("memset", P.VPs[1][:, :, 64:65], 1.0)
            E.op("memset", P.VPs[1][:, :, 129:130], 1.0)
            E.op("memset", P.KA[64:65, :], -1.0).inc(vS, 1)
            M.mark("b1vpones", vS)
        for hp in range(8):
            sl = lambda h=None: P.QKT[:, hp % 2, :, :]
            sln = P.QKT[:, (hp + 1) % 2, :, :]
            VP = P.VPs[hp % 2]
            VPn = P.VPs[(hp + 1) % 2]
            if nm == "sync":
                if hp == 0:
                    E.wait("cast_qkv_2")
                    E.wait("a3b_7")
                    _qkv_prefetch(0)
                    _qkv_prefetch(1)
                elif hp >= 2:
                    E.wait(f"b1qk_{hp - 2}")
                    E.wait(f"b1k_{hp - 2}")
                    _qkv_prefetch(hp)
            if nm == "gpsimd":
                for h2 in range(2):
                    hb = 64 * h2
                    E.wait(f"b1ropek_{hp}")
                    if h2 == 1:
                        E.wait(f"b1st_{hp}_0_{NTT - 1}_1")
                    P.qg_dma(E, P.QA[0:64, :], QR[hb:hb + 64, :])
                    P.qg_dma(E, P.KA[0:64, :], KR[hb:hb + 64, :])
                    E.wait(f"b1mrow_{hp}_{h2}")
                    P.qg_dma(E, P.QA[64:65, :], P.RB[0:1, :],
                             markname=f"b1qa_{hp}_{h2}")
                P.qg_dma(E, P.s_w1[hp], P.i_w1[hp], markname=f"cast_w1_{hp}",
                         q="qG2")
                P.qg_dma(E, P.s_w2[hp], P.i_w2[hp], markname=f"cast_w2_{hp}",
                         q="qG3")
                P.qg_dma(E, P.s_w3[hp], P.i_w3[hp], markname=f"cast_w3_{hp}",
                         q="qG2")
            if nm == "tensor":
                E.wait(f"L_qkv_{hp}")
                if hp == 0:
                    E.wait(f"b0F_{NTT - 1}")
                else:
                    E.wait(f"b1rle_{hp - 1}_1")
                for n in range(4):
                    for ic in range(NCC):
                        E.op("matmul", P.bank(n), sl()[:, ic, 0:128],
                             P.FM4[:, ic, 512 * n:512 * (n + 1)],
                             start=(ic == 0), stop=(ic == NCC - 1)).inc(
                                 tS, 1 if ic == NCC - 1 else 0)
                M.mark(f"b1q_{hp}", tS)
                if hp >= 1:
                    E.wait(f"b1ao_{hp - 1}_1")
                for n in range(4):
                    for ic in range(NCC):
                        E.op("matmul", P.bank(4 + n), sl()[:, ic, 128:256],
                             P.FM4[:, ic, 512 * n:512 * (n + 1)],
                             start=(ic == 0), stop=(ic == NCC - 1)).inc(
                                 tS, 1 if ic == NCC - 1 else 0)
                M.mark(f"b1k_{hp}", tS)
                E.wait(f"b1qe_{hp}")
                for n in range(4):
                    E.op("matmul", P.bank(n), P.permT[:],
                         QR[:, 512 * n:512 * (n + 1)], start=True, stop=True).inc(
                             tS, 1 if n == 3 else 0)
                M.mark(f"b1rq_{hp}", tS)
                E.wait(f"b1ke_{hp}")
                for n in range(4):
                    E.op("matmul", P.bank(4 + n), P.permT[:],
                         KR[:, 512 * n:512 * (n + 1)], start=True, stop=True).inc(
                             tS, 1 if n == 3 else 0)
                M.mark(f"b1rk_{hp}", tS)
                if hp == 0:
                    E.wait(f"b1ropeq_{hp}")
                    for tt in range(NTT):
                        b = tt // 4
                        off = 128 * (tt % 4)
                        for ic in range(NCC):
                            E.op("matmul", P.bank(b)[:, off:off + 128],
                                 P.FM4[:, ic, 128 * tt:128 * (tt + 1)],
                                 sl()[:, ic, 256:384],
                                 start=(ic == 0), stop=(ic == NCC - 1)).inc(
                                     tS, 1 if ic == NCC - 1 else 0)
                    M.mark(f"b1v_{hp}", tS)
                    M.mark(f"b1qk_{hp}", tS)
                hb = 0
                for qt in range(NTT):
                    grp = 4 * (qt % 2)
                    if qt == 0:
                        E.wait(f"b1ve_{hp}")
                        E.wait(f"b1ropeq_{hp}")
                    if qt == 1:
                        E.wait(f"b1ropek_{hp}")
                    if qt >= 2:
                        E.wait(f"b1mx_{hp}_0_{qt - 2}")
                    for n in range(4):
                        E.op("matmul", P.bank(grp + n),
                             QR[hb:hb + 64, 128 * qt:128 * (qt + 1)],
                             KR[hb:hb + 64, 512 * n:512 * (n + 1)],
                             start=True, stop=True).inc(tS, 1 if n == 3 else 0)
                    M.mark(f"b1s_{hp}_0_{qt}", tS)
                E.wait(f"b1cm_{hp}_0")
                E.op("transpose", P.bank(2)[0:16, 0:128],
                     P.MH[:, 0:16], P.identf[:]).inc(tS, 1)
                M.mark(f"b1mrT_{hp}_0", tS)
                E.wait(f"b1mre_{hp}_0")
                for qt in range(NTT):
                    bk = 4 + qt // 4
                    off = 128 * (qt % 4)
                    E.op("matmul", P.bank(bk)[0:1, off:off + 128],
                         P.sel16[:, qt:qt + 1], P.E16[0:16, 0:128],
                         start=True, stop=True).inc(tS, 1 if qt == NTT - 1 else 0)
                M.mark(f"b1mrr_{hp}_0", tS)
                for tk in range(NTT + 1):
                    if tk < NTT:
                        for qh in range(2):
                            bb = 2 * qh
                            if tk == 0:
                                E.wait(f"b1qa_{hp}_0")
                            else:
                                E.wait(f"b1exp_{hp}_0_{tk - 1}_{qh}")
                                E.wait(f"b1hx{qh}_{hp}_{tk - 1}")
                            for n in range(2):
                                E.op("matmul", P.bank(bb + n),
                                     P.KA[0:65, 128 * tk:128 * (tk + 1)],
                                     P.QA[0:65, 512 * (2 * qh + n):512 * (2 * qh + n + 1)],
                                     start=True, stop=True).inc(tS, 1 if n == 1 else 0)
                            M.mark(f"b1st_{hp}_0_{tk}_{qh}", tS)
                    if tk >= 1:
                        pk = tk - 1
                        ptb = P.HC if pk % 2 == 0 else P.PT2
                        for qh in range(2):
                            E.wait(f"b1exp_{hp}_0_{pk}_{qh}")
                            if pk == 0 and qh == 0 and hp == 0:
                                E.wait("b1vpones")
                            for n in range(2):
                                j = 2 * qh + n
                                E.op("matmul", P.bank(4 + j)[0:65, :],
                                     VP[:, pk, 0:65],
                                     ptb[:, 512 * j:512 * (j + 1)],
                                     start=(pk == 0), stop=(pk == NTT - 1),
                                     skip_group_check=True).inc(tS, 1 if n == 1 else 0)
                            M.mark(f"b1pv_{hp}_0_{pk}_{qh}", tS)
                    if tk < NTT:
                        for qh in range(2):
                            E.wait(f"b1exp_{hp}_0_{tk}_{qh}")
                            for n in range(2):
                                E.op("matmul", P.bank(2 * qh + n),
                                     QR[64:128, 128 * tk:128 * (tk + 1)],
                                     KR[64:128, 1024 * qh + 512 * n:1024 * qh + 512 * (n + 1)],
                                     start=True, stop=True).inc(tS, 1 if n == 1 else 0)
                            M.mark(f"b1s2{qh}_{hp}_{tk}", tS)
                E.wait(f"b1rs_{hp}_0")
                E.wait(f"b1hx0_{hp}_{NTT - 1}")
                E.wait(f"b1hx1_{hp}_{NTT - 1}")
                for n in range(4):
                    E.op("matmul", P.bank(n)[0:64, :], P.onesr[0:1, 0:64],
                         P.RB[0:1, 512 * n:512 * (n + 1)], start=True,
                         stop=True).inc(tS, 1 if n == 3 else 0)
                M.mark(f"b1rlb_{hp}_0", tS)
                E.wait(f"b1rle_{hp}_0")
                E.wait(f"b1cm_{hp}_1")
                E.op("transpose", P.bank(2)[0:16, 0:128],
                     P.MH[:, 0:16], P.identf[:]).inc(tS, 1)
                M.mark(f"b1mrT_{hp}_1", tS)
                E.wait(f"b1mre_{hp}_1")
                E.wait(f"b1ao_{hp}_0")
                for qt in range(NTT):
                    bk = 4 + qt // 4
                    off = 128 * (qt % 4)
                    E.op("matmul", P.bank(bk)[0:1, off:off + 128],
                         P.sel16[:, qt:qt + 1], P.E16[0:16, 0:128],
                         start=True, stop=True).inc(tS, 1 if qt == NTT - 1 else 0)
                M.mark(f"b1mrr_{hp}_1", tS)
                for tk in range(NTT + 1):
                    if tk < NTT:
                        for qh in range(2):
                            bb = 2 * qh
                            if tk == 0:
                                E.wait(f"b1qa_{hp}_1")
                            else:
                                E.wait(f"b1exp_{hp}_1_{tk - 1}_{qh}")
                                if qh == 0 and hp < 7:
                                    E.wait(f"b1vte_{hp + 1}_{tk - 1}")
                            for n in range(2):
                                E.op("matmul", P.bank(bb + n),
                                     P.KA[0:65, 128 * tk:128 * (tk + 1)],
                                     P.QA[0:65, 512 * (2 * qh + n):512 * (2 * qh + n + 1)],
                                     start=True, stop=True).inc(tS, 1 if n == 1 else 0)
                            M.mark(f"b1st_{hp}_1_{tk}_{qh}", tS)
                    if tk >= 1:
                        pk = tk - 1
                        ptb = P.HC if pk % 2 == 0 else P.PT2
                        for qh in range(2):
                            E.wait(f"b1exp_{hp}_1_{pk}_{qh}")
                            if pk == 0 and qh == 0:
                                E.wait(f"b1ao_{hp}_0")
                            for n in range(2):
                                j = 2 * qh + n
                                E.op("matmul", P.bank(4 + j)[0:65, :],
                                     VP[:, pk, 65:130],
                                     ptb[:, 512 * j:512 * (j + 1)],
                                     start=(pk == 0), stop=(pk == NTT - 1),
                                     skip_group_check=True).inc(tS, 1 if n == 1 else 0)
                            M.mark(f"b1pv_{hp}_1_{pk}_{qh}", tS)
                    if tk < NTT and hp < 7:
                        E.wait(f"b1exp_{hp}_1_{tk}_0")
                        if tk == 0:
                            E.wait(f"L_qkv_{hp + 1}")
                        for ic in range(NCC):
                            E.op("matmul", P.bank(0)[:, 0:128],
                                 P.FM4[:, ic, 128 * tk:128 * (tk + 1)],
                                 sln[:, ic, 256:384],
                                 start=(ic == 0), stop=(ic == NCC - 1),
                                 skip_group_check=True).inc(
                                     tS, 1 if ic == NCC - 1 else 0)
                        M.mark(f"b1vt_{hp + 1}_{tk}", tS)
                        if tk == NTT - 1:
                            M.mark(f"b1qk_{hp + 1}", tS)
                E.wait(f"b1rs_{hp}_1")
                if hp < 7:
                    E.wait(f"b1vte_{hp + 1}_{NTT - 1}")
                for n in range(4):
                    E.op("matmul", P.bank(n)[0:64, :], P.onesr[0:1, 0:64],
                         P.RB[0:1, 512 * n:512 * (n + 1)], start=True,
                         stop=True).inc(tS, 1 if n == 3 else 0)
                M.mark(f"b1rlb_{hp}_1", tS)
            if nm == "scalar":
                E.wait(f"b1q_{hp}")
                E.op("activation", QR, P.PS[:, 0:2048], AF.Identity,
                     bias=P.sv(hp, 10)).inc(aS, 1)
                M.mark(f"b1qe_{hp}", aS)
                E.wait(f"b1k_{hp}")
                E.op("activation", KR, P.PS[:, 2048:4096], AF.Identity,
                     scale=0.125, bias=P.sv(hp, 50)).inc(aS, 1)
                M.mark(f"b1ke_{hp}", aS)
                for h2 in range(2):
                    for tk in range(NTT):
                        ptb = P.HC if tk % 2 == 0 else P.PT2
                        for qh in range(2):
                            E.wait(f"b1st_{hp}_{h2}_{tk}_{qh}")
                            if tk >= 2:
                                E.wait(f"b1pv_{hp}_{h2}_{tk - 2}_{qh}")
                            E.op("activation", ptb[:, 1024 * qh:1024 * (qh + 1)],
                                 P.PS[:, 1024 * qh:1024 * (qh + 1)],
                                 AF.Exp).inc(aS, 1)
                            M.mark(f"b1exp_{hp}_{h2}_{tk}_{qh}", aS)
                    E.wait(f"b1pv_{hp}_{h2}_{NTT - 1}_1")
                    E.wait(f"b1qa_{hp}_{h2}")
                    E.op("activation", P.YS[0:1, :], P.PS[64:65, 2048:4096],
                         AF.Ln)
                    E.op("activation", P.RB[0:1, :], P.YS[0:1, :], AF.Exp,
                         scale=-1.0).inc(aS, 1)
                    M.mark(f"b1rs_{hp}_{h2}", aS)
            if nm == "vector":
                E.wait(f"b1rq_{hp}")
                E.op("tensor_mul", P.PT2[:], P.PS[:, 0:2048], P.RT[:, 1, :])
                E.op("tensor_mul", QR, QR, P.RT[:, 0, :])
                E.op("tensor_add", QR, QR, P.PT2[:]).inc(vS, 1)
                M.mark(f"b1ropeq_{hp}", vS)
                E.wait(f"b1rk_{hp}")
                E.op("tensor_mul", P.PT2[:], P.PS[:, 2048:4096], P.RT[:, 1, :])
                E.op("tensor_mul", KR, KR, P.RT[:, 0, :])
                E.op("tensor_add", KR, KR, P.PT2[:]).inc(vS, 1)
                M.mark(f"b1ropek_{hp}", vS)
                if hp == 0:
                    E.wait(f"b1v_{hp}")
                    for tt in range(NTT):
                        b = tt // 4
                        off = 128 * (tt % 4)
                        E.op("tensor_copy", VP[:, tt, 0:64],
                             P.bank(b)[:, off:off + 64])
                        E.op("tensor_copy", VP[:, tt, 65:129],
                             P.bank(b)[:, off + 64:off + 128]).inc(
                                 vS, 1 if tt == NTT - 1 else 0)
                    M.mark(f"b1ve_{hp}", vS)
                for qt in range(NTT):
                    E.wait(f"b1s_{hp}_0_{qt}")
                    E.op("reduce_max", P.MH[:, qt:qt + 1],
                         P.PS[:, 2048 * (qt % 2):2048 * (qt % 2) + 2048],
                         axis=AX.X).inc(vS, 1)
                    M.mark(f"b1mx_{hp}_0_{qt}", vS)
                M.mark(f"b1cm_{hp}_0", vS)
                E.wait(f"b1mrT_{hp}_0")
                E.op("tensor_copy", P.E16[0:16, 0:128],
                     P.bank(2)[0:16, 0:128]).inc(vS, 1)
                M.mark(f"b1mre_{hp}_0", vS)
                E.wait(f"b1mrr_{hp}_0")
                E.op("tensor_copy", P.RB[0:1, :], P.PS[0:1, 2048:4096]).inc(vS, 1)
                M.mark(f"b1mrow_{hp}_0", vS)
                for tk in range(NTT):
                    E.wait(f"b1s20_{hp}_{tk}")
                    E.wait(f"b1s21_{hp}_{tk}")
                    E.op("reduce_max", P.MH[:, tk:tk + 1],
                         P.PS[:, 0:2048], axis=AX.X).inc(vS, 1)
                    M.mark(f"b1hx0_{hp}_{tk}", vS)
                    M.mark(f"b1hx1_{hp}_{tk}", vS)
                M.mark(f"b1cm_{hp}_1", vS)
                E.wait(f"b1rlb_{hp}_0")
                E.op("tensor_copy", P.DEC[0:64, :], P.PS[0:64, 0:2048]).inc(vS, 1)
                M.mark(f"b1rle_{hp}_0", vS)
                dst = P.biga8[0:64, hp, :]
                E.op("tensor_mul", dst, P.PS[0:64, 2048:4096], P.DEC[0:64, :])
                E.op("tensor_scalar_add", dst, dst,
                     P.SV[0:64, hp, 12:13]).inc(vS, 1)
                M.mark(f"b1ao_{hp}_0", vS)
                E.wait(f"b1mrT_{hp}_1")
                E.op("tensor_copy", P.E16[0:16, 0:128],
                     P.bank(2)[0:16, 0:128]).inc(vS, 1)
                M.mark(f"b1mre_{hp}_1", vS)
                E.wait(f"b1mrr_{hp}_1")
                E.op("tensor_copy", P.RB[0:1, :], P.PS[0:1, 2048:4096]).inc(vS, 1)
                M.mark(f"b1mrow_{hp}_1", vS)
                if hp < 7:
                    for tk in range(NTT):
                        E.wait(f"b1vt_{hp + 1}_{tk}")
                        E.op("tensor_copy", VPn[:, tk, 0:64],
                             P.bank(0)[:, 0:64])
                        E.op("tensor_copy", VPn[:, tk, 65:129],
                             P.bank(0)[:, 64:128]).inc(vS, 1)
                        M.mark(f"b1vte_{hp + 1}_{tk}", vS)
                    M.mark(f"b1ve_{hp + 1}", vS)
                E.wait(f"b1rlb_{hp}_1")
                E.op("tensor_copy", P.DEC[0:64, :], P.PS[0:64, 0:2048]).inc(vS, 1)
                M.mark(f"b1rle_{hp}_1", vS)
                dst = P.biga8[64:128, hp, :]
                E.op("tensor_mul", dst, P.PS[0:64, 2048:4096], P.DEC[0:64, :])
                E.op("tensor_scalar_add", dst, dst,
                     P.SV[64:128, hp, 12:13]).inc(vS, 1)
                M.mark(f"b1ao_{hp}_1", vS)
                M.mark(f"b1ao2_{hp}", vS)
        if nm == "sync":
            for i in range(NTT):
                if i == 0:
                    E.wait(f"b0m_{NTT - 2}")
                    E.wait(f"b0m_{NTT - 1}")
                    E.wait(f"b0r_{NTT - 1}")
                if i >= 2:
                    E.wait(f"b3t_{i - 2}")
                    E.wait(f"b3ts_{i - 2}")
                E.op("dma_start", out=P.XT[:, i % 2, :],
                     in_=P.s_xn[128 * i:128 * (i + 1), :]).inc(qL, 16)
                M.mark(f"b3x_{i}", qL)
                E.wait(f"b3t_{i}")
                E.op("dma_start", out=P.s_t1[128 * i:128 * (i + 1), :],
                     in_=P.XT[:, i % 2, :]).inc(qL, 16)
                M.mark(f"b3ts_{i}", qL)
        if nm == "tensor":
            E.wait("L_owT")
            E.wait("b1ao2_7")
            E.wait("cast_out_b_f")
            for i in range(NTT):
                bb = 4 + 2 * (i % 2)
                if i >= 2:
                    E.wait(f"b3t_{i - 2}")
                for n in range(2):
                    for c in range(NCC):
                        E.op("matmul", P.bank(bb + n),
                             P.biga8[:, c, 128 * i:128 * (i + 1)],
                             P.SLA[:, c, 512 * n:512 * (n + 1)],
                             start=(c == 0), stop=False)
                    E.op("matmul", P.bank(bb + n), P.onesr[0:1, 0:128],
                         P.BR[0:1, 1, 512 * n:512 * (n + 1)],
                         start=False, stop=True).inc(tS, 1)
                M.mark(f"b3o_{i}", tS)
        if nm == "vector":
            for i in range(NTT):
                E.wait(f"b3o_{i}")
                E.wait(f"b3x_{i}")
                bb = 4 + 2 * (i % 2)
                E.op("tensor_add", P.XT[:, i % 2, :], P.XT[:, i % 2, :],
                     P.PS[:, 512 * bb:512 * (bb + 2)]).inc(vS, 1)
                M.mark(f"b3t_{i}", vS)

    def pC(self, E):
        P, nm = self, E.name
        qL, qA, qG, tS, vS, aS, pS = "qL", "qA", "qG", "tS", "vS", "aS", "pS"
        M = self.M
        OneK = 1.0 / 1024.0

        if nm == "sync":
            if "rt" in P.o_dbg:
                E.wait("a_tables")
                E.op("dma_start", out=P.o_dbg["rt"], in_=P.RT[:, :, :]).inc(qL, 16)
                M.mark("dbg_rt", qL)
            if "aon" in P.o_dbg:
                E.wait("b1ao2_7")
                E.op("dma_start", out=P.o_dbg["aon"], in_=P.BIGA[:, :]).inc(qL, 16)
                M.mark("dbg_aon", qL)
            if "t1" in P.o_dbg:
                E.wait(f"b3ts_{NTT - 1}")
                E.op("dma_start", out=P.o_dbg["t1"], in_=P.s_t1).inc(qL, 16)
                M.mark("dbg_t1", qL)
            for i in range(NTT):
                E.wait(f"b3ts_{i}")
                if i >= 2:
                    E.wait(f"c0T_{i - 2}")
                E.op("dma_start", out=P.XT[:, i % 2, :],
                     in_=P.s_t1[128 * i:128 * (i + 1), :]).inc(qL, 16)
                M.mark(f"c0x_{i}", qL)
        if nm == "scalar":
            for i in range(NTT):
                E.wait(f"c0x_{i}")
                C = P.STC
                E.op("activation", P.DUMP[:], P.XT[:, i % 2, :], AF.Square,
                     accum_out=C[:, 46:47]).inc(aS, 1)
                if i >= 1:
                    E.wait(f"c0m_{i - 1}")
                E.op("activation", C[:, 47:48], C[:, 46:47], AF.Ln,
                     scale=OneK, bias=P.SV[:, 0, 52:53])
                E.op("activation", C[:, 47:48], C[:, 47:48], AF.Exp,
                     scale=-0.5).inc(aS, 1)
                M.mark(f"c0r_{i}", aS)
        if nm == "vector":
            for i in range(NTT):
                E.wait(f"c0r_{i}")
                if i >= 2:
                    E.wait(f"c0T_{i - 2}")
                u = P.UST[:, i % 2, 0:1024]
                E.op("tensor_scalar", u, P.XT[:, i % 2, :], P.STC[:, 47:48], None,
                     ALU.mult)
                E.op("tensor_mul", u, u, P.NRM[:, 1, :]).inc(vS, 1)
                M.mark(f"c0m_{i}", vS)
                if i >= 1:
                    E.wait(f"c0T_{i - 1}")
                    b = (i - 1) % 2
                    E.op("tensor_copy",
                         P.FM4[:, :, 128 * (i - 1):128 * i],
                         P.bank(b).bitcast(BF16).rearrange("p (c t) -> p c t", c=8)).inc(vS, 1)
                    M.mark(f"c0F_{i - 1}", vS)
                    E.wait(f"c0g_{i - 1}")
                    if (i - 1) == 0 and "rt" in P.o_dbg:
                        E.wait("dbg_rt")
                    E.op("tensor_copy", P.GL[:, 8 * (i - 1):8 * i],
                         P.bank(4)[:, 128 * ((i - 1) % 4):128 * ((i - 1) % 4) + 8]).inc(vS, 1)
                    M.mark(f"c0ge_{i - 1}", vS)
            E.wait(f"c0T_{NTT - 1}")
            E.op("tensor_copy", P.FM4[:, :, 128 * (NTT - 1):128 * NTT],
                 P.bank((NTT - 1) % 2).bitcast(BF16).rearrange("p (c t) -> p c t", c=8)).inc(vS, 1)
            M.mark(f"c0F_{NTT - 1}", vS)
            E.wait(f"c0g_{NTT - 1}")
            E.op("tensor_copy", P.GL[:, 8 * (NTT - 1):8 * NTT],
                 P.bank(4)[:, 128 * ((NTT - 1) % 4):128 * ((NTT - 1) % 4) + 8]).inc(vS, 1)
            M.mark(f"c0ge_{NTT - 1}", vS)
            G = P.GL
            L3 = G[:, 0:128].rearrange("p (g e) -> p g e", e=8)
            m1 = G[:, 128:144]
            m1b = G[:, 128:144].rearrange("p g -> p g").unsqueeze(2).broadcast_to([128, 16, 8])
            m2 = G[:, 144:160]
            m2b = G[:, 144:160].unsqueeze(2).broadcast_to([128, 16, 8])
            s1 = G[:, 160:176]
            s2 = G[:, 176:192]
            s1b = G[:, 160:176].unsqueeze(2).broadcast_to([128, 16, 8])
            s2b = G[:, 176:192].unsqueeze(2).broadcast_to([128, 16, 8])
            ge1 = G[:, 192:320].rearrange("p (g e) -> p g e", e=8)
            tm = G[:, 320:448].rearrange("p (g e) -> p g e", e=8)
            comb = G[:, 448:576].rearrange("p (g e) -> p g e", e=8)
            dm = G[:, 576:592]
            E.op("tensor_reduce", m1, L3, AX.X, ALU.max)
            E.op("tensor_tensor", ge1, L3, m1b, ALU.is_ge)
            E.op("scalar_tensor_tensor", tm, ge1, -1e30, L3, ALU.mult, ALU.add)
            E.op("tensor_reduce", m2, tm, AX.X, ALU.max)
            E.op("tensor_sub", dm, m1, m2).inc(vS, 1)
            M.mark("c0gate1", vS)
            E.wait("c0sig")
            E.op("tensor_scalar", s2, s1, -1.0, 1.0, ALU.mult, ALU.add)
            E.op("tensor_tensor", comb, ge1, s1b, ALU.mult)
            E.op("tensor_tensor", tm, tm, m2b, ALU.is_ge)
            E.op("tensor_tensor", tm, tm, s2b, ALU.mult)
            E.op("tensor_add", comb, comb, tm).inc(vS, 1)
            M.mark("c0comb", vS)
        if nm == "scalar":
            E.wait("c0gate1")
            E.op("activation", P.GL[:, 160:176], P.GL[:, 576:592], AF.Sigmoid).inc(aS, 1)
            M.mark("c0sig", aS)
        if nm == "tensor":
            for i in range(NTT):
                E.wait(f"c0m_{i}")
                if i >= 2:
                    E.wait(f"c0F_{i - 2}")
                b = i % 2
                for c in range(NCC):
                    ap = P.bank(b).bitcast(BF16)[:, 128 * c:128 * (c + 1)]
                    E.op("transpose", ap, P.UST[:, i % 2, 128 * c:128 * (c + 1)],
                         P.identb[:]).inc(tS, 1 if c == NCC - 1 else 0)
                M.mark(f"c0T_{i}", tS)
                if i >= 4:
                    E.wait(f"c0ge_{i - 4}")
                E.wait(f"c0F_{i}")
                off = 128 * (i % 4)
                for c in range(NCC):
                    E.op("matmul", P.bank(4)[:, off:off + 8],
                         P.FM4[:, c, 128 * i:128 * (i + 1)],
                         P.GW[:, c, :], start=(c == 0), stop=(c == NCC - 1)).inc(
                             tS, 1 if c == NCC - 1 else 0)
                M.mark(f"c0g_{i}", tS)

        step = 0
        for e in range(NE):
            for th in range(2):
                s_w1, s_w2, s_w3 = step, step + 1, step + 2
                sl1 = P.slaV if s_w1 % 2 == 0 else P.slbV
                sl2 = P.slaV if s_w2 % 2 == 0 else P.slbV
                sl3 = P.SLA if s_w3 % 2 == 0 else P.SLB
                if nm == "sync":
                    if s_w1 >= 2:
                        E.wait(f"c1slab_{s_w1 - 2}")
                    else:
                        E.wait(f"b3o_{NTT - 1}")
                    E.wait(f"cast_w1_{e}")
                    sl1d = P.SLA if s_w1 % 2 == 0 else P.SLB
                    for c in range(NCC):
                        E.op("dma_start", out=sl1d[:, :, :].rearrange("p a b -> p (a b)")
                             .rearrange("p (c t) -> p c t", c=8)[:, c, :],
                             in_=P.s_w1[e][:, 128 * c:128 * (c + 1)], transpose=True).inc(qL, 16)
                    M.mark(f"L_s_{s_w1}", qL)
                    if s_w2 >= 2:
                        E.wait(f"c1slab_{s_w2 - 2}")
                    else:
                        E.wait(f"b3o_{NTT - 1}")
                    E.wait(f"cast_w2_{e}")
                    sl2d = P.SLA if s_w2 % 2 == 0 else P.SLB
                    for c in range(NCC):
                        E.op("dma_start", out=sl2d[:, :, :].rearrange("p a b -> p (a b)")
                             .rearrange("p (c t) -> p c t", c=8)[:, c, :],
                             in_=P.s_w2[e][:, 128 * c:128 * (c + 1)], transpose=True).inc(qL, 16)
                    M.mark(f"L_s_{s_w2}", qL)
                    E.wait(f"c1slab_{s_w3 - 2}")
                    E.wait(f"cast_w3_{e}")
                    for c in range(16):
                        E.op("dma_start", out=sl3[:, c, :],
                             in_=P.s_w3[e][:, 128 * c:128 * (c + 1)], transpose=True).inc(qL, 16)
                    M.mark(f"L_s_{s_w3}", qL)
                if nm == "scalar":
                    if e == 0 and th == 0 and "aon" in P.o_dbg:
                        E.wait("dbg_aon")
                    for hc in range(16):
                        E.wait(f"c1g1_{step}_{hc}")
                        bb = 2 * (hc % 2)
                        r1 = 13 + 2 * e + hc // 8
                        E.op("activation", P.biga16[:, hc, :],
                             P.PS[:, 512 * bb:512 * (bb + 2)], AF.Silu,
                             bias=P.sv(hc % 8, r1)).inc(aS, 1)
                        M.mark(f"c1s1_{step}_{hc}", aS)
                if th == 0:
                    if nm == "gpsimd":
                        P.qg_dma(E, P.BR[0:1, 2, :], P.i_b3_all[e].unsqueeze(0),
                                 markname=f"cast_b3_{e}", q="qG3")
                    if nm == "tensor":
                        E.wait("c0comb")
                        if e >= 1:
                            E.wait(f"c1cbF_{e - 1}")
                            E.wait(f"c1oe_{step - 3}_6")
                            E.wait(f"c1oe_{step - 3}_7")
                        combe = P.GL[:, 448:576].rearrange("p (g e) -> p g e", e=8)[:, :, e]
                        E.op("transpose", P.bank(4)[0:16, 0:128],
                             combe, P.identf[:]).inc(tS, 1)
                        M.mark(f"c1cbT_{e}", tS)
                        E.wait(f"c1cbe_{e}")
                        for qt in range(NTT):
                            bk = 4 + qt // 4
                            off = 128 * (qt % 4)
                            E.op("matmul", P.bank(bk)[0:1, off:off + 128],
                                 P.sel16[:, qt:qt + 1], P.E16[0:16, 0:128],
                                 start=True, stop=True).inc(tS, 1 if qt == NTT - 1 else 0)
                        M.mark(f"c1cbr_{e}", tS)
                        E.wait(f"c1cbre_{e}")
                        for n in range(4):
                            E.op("matmul", P.bank(4 + n), P.onesr[0:1, :],
                                 P.CBR[0:1, 512 * n:512 * (n + 1)], start=True,
                                 stop=True).inc(tS, 1 if n == 3 else 0)
                        M.mark(f"c1cbb_{e}", tS)
                    if nm == "vector":
                        E.wait(f"c1cbT_{e}")
                        E.op("tensor_copy", P.E16[0:16, 0:128],
                             P.bank(4)[0:16, 0:128]).inc(vS, 1)
                        M.mark(f"c1cbe_{e}", vS)
                        E.wait(f"c1cbr_{e}")
                        E.op("tensor_copy", P.CBR[0:1, :],
                             P.PS[0:1, 2048:4096]).inc(vS, 1)
                        M.mark(f"c1cbre_{e}", vS)
                        E.wait(f"c1cbb_{e}")
                        E.op("tensor_copy", P.HC[:], P.PS[:, 2048:4096]).inc(vS, 1)
                        M.mark(f"c1cbF_{e}", vS)
                if nm == "tensor":
                    E.wait(f"L_s_{s_w1}")
                    if e == 0 and th == 0:
                        E.wait(f"c0F_{NTT - 1}")
                        E.wait(f"c0g_{NTT - 1}")
                        E.wait(f"c1cbF_{e}")
                    for hc in range(16):
                        bb = 2 * (hc % 2)
                        if step == 0 and hc < 2:
                            pass
                        elif hc >= 2:
                            E.wait(f"c1s1_{step}_{hc - 2}")
                        else:
                            E.wait(f"c1h_{step - 3}_{14 + hc}")
                        for n in range(2):
                            for c in range(NCC):
                                E.op("matmul", P.bank(bb + n),
                                     sl1[:, c, 128 * hc:128 * (hc + 1)],
                                     P.FM4[:, c, 1024 * th + 512 * n:1024 * th + 512 * (n + 1)],
                                     start=(c == 0), stop=(c == NCC - 1)).inc(
                                         tS, 1 if c == NCC - 1 else 0)
                        M.mark(f"c1g1_{step}_{hc}", tS)
                    M.mark(f"c1slab_{s_w1}", tS)
                    E.wait(f"L_s_{s_w2}")
                    for hc in range(16):
                        bb = 2 * (hc % 2)
                        if hc >= 2:
                            E.wait(f"c1h_{step}_{hc - 2}")
                        else:
                            E.wait(f"c1s1_{step}_{14 + hc}")
                        for n in range(2):
                            for c in range(NCC):
                                E.op("matmul", P.bank(bb + n),
                                     sl2[:, c, 128 * hc:128 * (hc + 1)],
                                     P.FM4[:, c, 1024 * th + 512 * n:1024 * th + 512 * (n + 1)],
                                     start=(c == 0), stop=(c == NCC - 1)).inc(
                                         tS, 1 if c == NCC - 1 else 0)
                        M.mark(f"c1l2_{step}_{hc}", tS)
                    M.mark(f"c1slab_{s_w2}", tS)
                    E.wait(f"L_s_{s_w3}")
                    E.wait(f"cast_b3_{e}")
                    E.wait(f"c1cb_{step}_15")
                    for j in range(8):
                        i = 8 * th + j
                        bb = 4 + 2 * (j % 2)
                        if j >= 2:
                            E.wait(f"c1oe_{step}_{j - 2}")
                        elif th == 0 and e >= 1:
                            E.wait(f"c1oe_{step - 3}_{6 + j}")
                        elif th == 1:
                            E.wait(f"c1oe_{step - 3}_{6 + j}")
                        elif e == 0 and th == 0:
                            E.wait(f"c1cbF_{e}")
                        for n in range(2):
                            for hc in range(16):
                                E.op("matmul", P.bank(bb + n),
                                     P.biga16[:, hc, 128 * j:128 * (j + 1)],
                                     sl3[:, hc, 512 * n:512 * (n + 1)],
                                     start=(hc == 0), stop=False)
                            E.op("matmul", P.bank(bb + n),
                                 P.CBR[0:1, 128 * i:128 * (i + 1)],
                                 P.BR[0:1, 2, 512 * n:512 * (n + 1)],
                                 start=False, stop=True).inc(tS, 1)
                        M.mark(f"c1o_{step}_{j}", tS)
                    M.mark(f"c1slab_{s_w3}", tS)
                if nm == "vector":
                    for hc in range(16):
                        E.wait(f"c1l2_{step}_{hc}")
                        bb = 2 * (hc % 2)
                        r2 = 29 + 2 * e + hc // 8
                        E.op("scalar_tensor_tensor", P.biga16[:, hc, :],
                             P.PS[:, 512 * bb:512 * (bb + 2)], P.sv(hc % 8, r2),
                             P.biga16[:, hc, :], ALU.add, ALU.mult).inc(vS, 1)
                        M.mark(f"c1h_{step}_{hc}", vS)
                        E.op("tensor_mul", P.biga16[:, hc, :], P.biga16[:, hc, :],
                             P.HC[:, 1024 * th:1024 * (th + 1)]).inc(vS, 1)
                        M.mark(f"c1cb_{step}_{hc}", vS)
                    for j in range(8):
                        E.wait(f"c1o_{step}_{j}")
                        if j >= 2:
                            E.wait(f"c1acc_{step}_{j - 2}")
                        elif step >= 3:
                            E.wait(f"c1acc_{step - 3}_{6 + j}")
                        elif e == 0 and th == 0:
                            E.wait(f"c0x_{NTT - 1}")
                        bb = 4 + 2 * (j % 2)
                        E.op("tensor_copy", P.XT[:, j % 2, :],
                             P.PS[:, 512 * bb:512 * (bb + 2)]).inc(vS, 1)
                        M.mark(f"c1oe_{step}_{j}", vS)
                if nm == "gpsimd":
                    for j in range(8):
                        i = 8 * th + j
                        E.wait(f"c1oe_{step}_{j}")
                        if e == 0 and th == 0 and j == 0:
                            E.wait(f"b3ts_{NTT - 1}")
                            if "t1" in P.o_dbg:
                                E.wait("dbg_t1")
                        P.qg_dma(E, P.s_t1[128 * i:128 * (i + 1), :],
                                 P.XT[:, j % 2, :], accum=ALU.add,
                                 markname=f"c1acc_{step}_{j}")
                step += 3
        if nm == "gpsimd":
            M.mark("c1acc_all", qG)

        if nm == "gpsimd":
            for i in range(NTT):
                if i >= 2:
                    E.wait(f"c2o_{i - 2}")
                P.qg_dma(E, P.DEC[:, 1024 * (i % 2):1024 * (i % 2 + 1)],
                         P.s_t1[128 * i:128 * (i + 1), :], markname=f"c2lt_{i}")
        if nm == "sync":
            E.wait("c1acc_all")
            for i in range(NTT):
                if i >= 2:
                    E.wait(f"c2o_{i - 2}")
                    E.wait(f"c2os_{i - 2}")
                E.op("dma_start", out=P.XT[:, i % 2, :],
                     in_=P.i_x[128 * i:128 * (i + 1), :]).inc(qL, 16)
                M.mark(f"c2l_{i}", qL)
                E.wait(f"c2o_{i}")
                E.op("dma_start", out=P.o_out[128 * i:128 * (i + 1), :],
                     in_=P.DEC[:, 1024 * (i % 2):1024 * (i % 2 + 1)]).inc(qL, 16)
                M.mark(f"c2os_{i}", qL)
            sem, cnt = M.at(f"c2os_{NTT - 1}")
            E.wait_abs(qL, cnt)
        if nm == "scalar":
            E.wait(f"a4ss_{NTT - 1}")
            E.wait("c1acc_all")
            for i in range(NTT):
                if i >= 2:
                    E.wait(f"c2o_{i - 2}")
                E.op("dma_start", out=P.ST[:, i % 2, :],
                     in_=P.s_s[128 * i:128 * (i + 1), :]).inc(qA, 16)
                M.mark(f"c2ls_{i}", qA)
                E.wait(f"c2lt_{i}")
                E.op("activation", P.HS[:, 1024 * (i % 2):1024 * (i % 2 + 1)],
                     P.DEC[:, 1024 * (i % 2):1024 * (i % 2 + 1)], AF.Silu).inc(aS, 1)
                M.mark(f"c2g_{i}", aS)
        if nm == "sync" and P.o_dbg:
            sem, cnt = M.at(f"c2os_{NTT - 1}")
            E.wait_abs(qL, cnt)
            srcmap = {"xn": P.s_xn, "s": P.s_s, "t2": P.s_t1,
                      "sgv": P.s_sgv, "comb": P.GL[:, 448:576],
                      "xn2": P.FM4[:, :, :], "qr": P.INP[:],
                      "kr": P.HS[:], "mh": P.MH[:, :],
                      "sc": P.SC[:, :], "sr": P.SR[:, :]}
            ndbg = 0
            for nmv, ap in P.o_dbg.items():
                if nmv in srcmap:
                    E.op("dma_start", out=ap, in_=srcmap[nmv]).inc(qL, 16)
                    ndbg += 1
            E.wait_abs(qL, cnt + 16 * ndbg)
        if nm == "vector":
            for i in range(NTT):
                E.wait(f"c2g_{i}")
                E.wait(f"c2l_{i}")
                E.wait(f"c2ls_{i}")
                half = P.DEC[:, 1024 * (i % 2):1024 * (i % 2 + 1)]
                E.op("tensor_mul", half, P.ST[:, i % 2, :],
                     P.HS[:, 1024 * (i % 2):1024 * (i % 2 + 1)])
                E.op("tensor_add", half, half, P.XT[:, i % 2, :]).inc(vS, 1)
                M.mark(f"c2o_{i}", vS)

    def emit_engine(self, name, handle):
        E = Eng(self, name)
        E.h = handle
        self.p0(E)
        self.pA(E)
        self.pB(E)
        self.pC(E)

    def run_passes(self):
        for dry in (True, True):
            self.dry = True
            self.M.reset_pass(True)
            self.qstate = {"qL": {"mark": 0, "barrier": 0}, "qA": {"mark": 0, "barrier": 0}}
            for en in ["sync", "gpsimd", "scalar", "tensor", "vector"]:
                self._qg = {"qG": 0, "qG2": 0, "qG3": 0}
                self.emit_engine(en, None)
        self.dry = False
        self.M.reset_pass(False)
        self.qstate = {"qL": {"mark": 0, "barrier": 0}, "qA": {"mark": 0, "barrier": 0}}
        nc = self.nc
        with nc.Block() as block:
            def mk(en):
                def f(h):
                    self._qg = {"qG": 0, "qG2": 0, "qG3": 0}
                    self.emit_engine(en, h)
                return f
            block._start_engine(mk("sync"), mybir.EngineType.SP)
            block.gpsimd(mk("gpsimd"))
            block.scalar(mk("scalar"))
            block.tensor(mk("tensor"))
            block.vector(mk("vector"))

    def build(self):
        self.nc = bass.Bass("TRN2", target_bir_lowering=False)
        self.declare()
        self.alloc()
        self.run_passes()
        return self.nc


def _host_constants():
    bf = ml_dtypes.bfloat16
    c = {}
    c["identb"] = np.eye(128, dtype=bf)
    c["identf"] = np.eye(128, dtype=np.float32)
    c["sel16"] = np.eye(16, dtype=bf)
    c["onesr"] = np.ones((1, 128), dtype=bf)
    c["onesrf"] = np.ones((1, 128), dtype=np.float32)
    c["onesc"] = np.ones((128, 1), dtype=bf)
    perm = np.zeros((128, 128), dtype=np.float32)
    for d in range(128):
        base = (d // 64) * 64
        r = d % 64
        if r < 32:
            perm[d, base + r + 32] = -1.0
        else:
            perm[d, base + r - 32] = 1.0
    c["permT"] = perm.T.astype(bf)
    c["onesq"] = np.ones((128, 128), dtype=bf)
    s2 = np.zeros((128, 2), dtype=bf)
    s2[0:64, 0] = 1
    s2[64:128, 1] = 1
    c["sel2"] = s2
    c["jcol"] = (np.arange(128) % 32).astype(np.float32).reshape(128, 1)
    c["trow"] = np.arange(T, dtype=np.float32).reshape(1, T)
    sc = np.zeros((1, 8), dtype=np.float32)
    sc[0, 0] = 10000.0
    c["scals"] = sc
    return c


def _stack_vecs(inp):
    rows = [
        inp["L0_attn_ln_g"], inp["L0_attn_ln_b"], inp["L0_attn_b_conv"],
        inp["L0_attn_b_delta"], inp["L0_attn_b_B"], inp["L0_attn_b_C"],
        inp["L0_ffn_norm1_w"], inp["L0_ffn_norm2_w"],
        inp["L0_attn_b_in"][0:1024], inp["L0_attn_b_in"][1024:2048],
        inp["L0_ffn_qkv_b"][0:1024], inp["L0_ffn_qkv_b"][1024:2048],
        inp["L0_ffn_qkv_b"][2048:3072],
    ]
    for e in range(NE):
        rows += [inp["L0_ffn_b1"][e][0:1024], inp["L0_ffn_b1"][e][1024:2048]]
    for e in range(NE):
        rows += [inp["L0_ffn_b2"][e][0:1024], inp["L0_ffn_b2"][e][1024:2048]]
    for k in range(4):
        rows.append(inp["L0_attn_w_conv"][:, 0, k])
    return np.ascontiguousarray(np.stack(rows, 0).astype(np.float32, copy=False))


_NC_CACHE = {}


def kernel(**inputs):
    key = "v1"
    if key not in _NC_CACHE:
        _NC_CACHE[key] = MK().build()
    nc = _NC_CACHE[key]
    consts = _host_constants()
    vecs = _stack_vecs(inputs)
    base = {
        "w_in": inputs["L0_attn_w_in"], "w_delta": inputs["L0_attn_w_delta"],
        "w_B": inputs["L0_attn_w_B"], "w_C": inputs["L0_attn_w_C"],
        "w_out": inputs["L0_attn_w_out"], "qkv_w": inputs["L0_ffn_qkv_w"],
        "out_w": inputs["L0_ffn_out_w"], "gate_w": inputs["L0_ffn_gate_w"],
        "w1": inputs["L0_ffn_w1"], "w2": inputs["L0_ffn_w2"], "w3": inputs["L0_ffn_w3"],
        "vecs_a": vecs, "b_out_m": inputs["L0_attn_b_out"],
        "out_b_f": inputs["L0_ffn_out_b"], "b3_all": inputs["L0_ffn_b3"],
        **consts,
    }
    base = {k: np.ascontiguousarray(v) for k, v in base.items()}
    x = np.ascontiguousarray(inputs["L0_attn_x"], dtype=np.float32)
    in_maps = [dict(base, x_in=np.ascontiguousarray(x[b])) for b in range(8)]
    res = run_bass_kernel_spmd(nc, in_maps, core_ids=list(range(8)))
    out = np.stack([np.asarray(r["out"], dtype=np.float32) for r in res.results], axis=0)
    return out
